# version 03
# baseline (speedup 1.0000x reference)
import numpy as np
import ml_dtypes
import concourse.bass as bass
import concourse.mybir as mybir
from concourse.bass_utils import run_bass_kernel_spmd
from contextlib import ExitStack

F32 = mybir.dt.float32
BF16 = mybir.dt.bfloat16
AF = mybir.ActivationFunctionType
AX = mybir.AxisListType
ALU = mybir.AluOpType

S, D = 8192, 2048
H, DH = 16, 128
DF = 1408
NHT = 11
NCH = 16
NBLK = 8
QW = 256
SCALE = float(1.0 / np.sqrt(128.0))
EPS = 1e-5

DEBUG = 0


class Sem:
    def __init__(self, ctx, nc, name):
        self.h = ctx.enter_context(nc.semaphore(name))
        self.n = 0

    def inc(self, instr, k=1):
        instr.then_inc(self.h, k)
        self.n += k
        return instr


class Ring:

    def __init__(self, depth):
        self.depth = depth
        self.pv = []
        self.cv = []

    def wait_free(self, eng):
        i = len(self.pv)
        if i >= self.depth:
            s, v = self.cv[i - self.depth]
            eng.wait_ge(s.h, v)

    def produced(self, sem):
        self.pv.append((sem, sem.n))

    def wait_full(self, eng):
        s, v = self.pv[len(self.cv)]
        eng.wait_ge(s.h, v)

    def consumed(self, sem):
        self.cv.append((sem, sem.n))

    @property
    def np_(self):
        return len(self.pv)

    @property
    def nc_(self):
        return len(self.cv)


def build():
    nc = bass.Bass("TRN2", target_bir_lowering=False)

    xb = nc.dram_tensor("xb", [S, D], F32, kind="ExternalInput").ap()
    xband = nc.dram_tensor("xband", [2048, D], F32, kind="ExternalInput").ap()
    xmine = nc.dram_tensor("xmine", [2048, D], F32, kind="ExternalInput").ap()
    masksI = nc.dram_tensor("masks", [NBLK, 128, 2048], BF16, kind="ExternalInput").ap()
    ident = nc.dram_tensor("ident", [128, 128], BF16, kind="ExternalInput").ap()
    wq = nc.dram_tensor("wq", [D, D], F32, kind="ExternalInput").ap()
    wk = nc.dram_tensor("wk", [D, D], F32, kind="ExternalInput").ap()
    wv = nc.dram_tensor("wv", [D, D], F32, kind="ExternalInput").ap()
    wo = nc.dram_tensor("wo", [D, D], F32, kind="ExternalInput").ap()
    bq = nc.dram_tensor("bq", [1, D], F32, kind="ExternalInput").ap()
    bk = nc.dram_tensor("bk", [1, D], F32, kind="ExternalInput").ap()
    bv = nc.dram_tensor("bv", [1, D], F32, kind="ExternalInput").ap()
    bo = nc.dram_tensor("bo", [1, D], F32, kind="ExternalInput").ap()
    sw1 = nc.dram_tensor("sw1", [2, D, DF], F32, kind="ExternalInput").ap()
    sb1 = nc.dram_tensor("sb1", [2, DF], F32, kind="ExternalInput").ap()
    sw2 = nc.dram_tensor("sw2", [2, DF, D], F32, kind="ExternalInput").ap()
    sb2 = nc.dram_tensor("sb2", [2, D], F32, kind="ExternalInput").ap()
    rw1 = nc.dram_tensor("rw1", [32, D, DF], F32, kind="ExternalInput").ap()
    rb1 = nc.dram_tensor("rb1", [32, DF], F32, kind="ExternalInput").ap()
    rw2 = nc.dram_tensor("rw2", [32, DF, D], F32, kind="ExternalInput").ap()
    rb2 = nc.dram_tensor("rb2", [32, D], F32, kind="ExternalInput").ap()
    rww = nc.dram_tensor("rww", [D, 32], F32, kind="ExternalInput").ap()
    rwb = nc.dram_tensor("rwb", [1, 32], F32, kind="ExternalInput").ap()

    y = nc.dram_tensor("y", [2048, D], F32, kind="ExternalOutput").ap()

    kscr = nc.dram_tensor("kscr", [H, 128, S], BF16, kind="Internal").ap()
    vscr = nc.dram_tensor("vscr", [H, 128, S], BF16, kind="Internal").ap()
    qscr = nc.dram_tensor("qscr", [H, 128, 2048], BF16, kind="Internal").ap()
    ascr = nc.dram_tensor("ascr", [H, 128, 2048], BF16, kind="Internal").ap()
    h2ts = nc.dram_tensor("h2ts", [NCH, 128, 2048], BF16, kind="Internal").ap()
    x1s = nc.dram_tensor("x1s", [16, 128, D], F32, kind="Internal").ap()
    wscr1 = nc.dram_tensor("wscr1", [34, 6, 128, 4096], BF16, kind="Internal").ap()
    wscr2 = nc.dram_tensor("wscr2", [34, 128, NHT * D], BF16, kind="Internal").ap()

    if DEBUG >= 1:
        dbg_k = nc.dram_tensor("dbg_k", [H, 128, S], BF16, kind="ExternalOutput").ap()
        dbg_v = nc.dram_tensor("dbg_v", [H, 128, S], BF16, kind="ExternalOutput").ap()
        dbg_q = nc.dram_tensor("dbg_q", [H, 128, 2048], BF16, kind="ExternalOutput").ap()
    if DEBUG >= 2:
        dbg_at = nc.dram_tensor("dbg_at", [H, 128, 2048], BF16, kind="ExternalOutput").ap()
    if DEBUG >= 3:
        dbg_x1 = nc.dram_tensor("dbg_x1", [16, 128, D], F32, kind="ExternalOutput").ap()
        dbg_h2t = nc.dram_tensor("dbg_h2t", [NCH, 128, 2048], BF16, kind="ExternalOutput").ap()
        dbg_g = nc.dram_tensor("dbg_g", [32, 2048], BF16, kind="ExternalOutput").ap()

    ctx = ExitStack()
    e = ctx.enter_context
    with ctx:
        sOnesB = e(nc.sbuf_tensor("sOnesB", [1, 128], BF16))
        sOnesF = e(nc.sbuf_tensor("sOnesF", [1, 128], F32))
        sOnesC = e(nc.sbuf_tensor("sOnesC", [128, 1], BF16))
        sIdent = e(nc.sbuf_tensor("sIdent", [128, 128], BF16))
        sIdentF = e(nc.sbuf_tensor("sIdentF", [128, 128], F32))
        sBqT = e(nc.sbuf_tensor("sBqT", [128, 16], F32))
        sBkT = e(nc.sbuf_tensor("sBkT", [128, 16], F32))
        sBvR = e(nc.sbuf_tensor("sBvR", [1, D], BF16))
        sBoR = e(nc.sbuf_tensor("sBoR", [1, D], BF16))
        sRb1T = e(nc.sbuf_tensor("sRb1T", [128, NHT * 32], F32))
        sHsb1T = e(nc.sbuf_tensor("sHsb1T", [128, NHT * 2], F32))
        sSb2c = e(nc.sbuf_tensor("sSb2c", [1, D], BF16))
        sRb2B = e(nc.sbuf_tensor("sRb2B", [32, D], BF16))
        sRwB = e(nc.sbuf_tensor("sRwB", [128, NCH * 32], BF16))
        sRwbT = e(nc.sbuf_tensor("sRwbT", [32, 1], F32))
        sGatesT = e(nc.sbuf_tensor("sGatesT", [32, 2048], BF16))
        sGRow = e(nc.sbuf_tensor("sGRow", [1, 512], BF16))
        sMV = e(nc.sbuf_tensor("sMV", [128, 2], F32))
        sStat = e(nc.sbuf_tensor("sStat", [128, 24], F32))
        sT0 = e(nc.sbuf_tensor("sT0", [128, 1], F32))
        sT1 = e(nc.sbuf_tensor("sT1", [128, 1], F32))
        sT2 = e(nc.sbuf_tensor("sT2", [128, 1], F32))
        sMx = e(nc.sbuf_tensor("sMx", [1, 24], F32))
        sSc0 = e(nc.sbuf_tensor("sSc0", [1, 1], F32))
        sSc1 = e(nc.sbuf_tensor("sSc1", [1, 1], F32))
        sBias = e(nc.sbuf_tensor("sBias", [128, 1], F32))

        dS = Sem(ctx, nc, "dS")
        dC = Sem(ctx, nc, "dC")
        tM = Sem(ctx, nc, "tM")
        vM = Sem(ctx, nc, "vM")
        sM = Sem(ctx, nc, "sM")
        pM = Sem(ctx, nc, "pM")
        ALL = (tM, vM, sM, pM, dS, dC)

        block = e(nc.Block())

        def on(name):
            dec = getattr(block, name)

            def run(fn):
                dec(fn)
            return run

        def barrier():
            counts = [(s, s.n) for s in ALL]
            for eng in ("tensor", "vector", "scalar", "gpsimd", "sync"):
                @on(eng)
                def _(E, counts=counts):
                    for s, v in counts:
                        if v:
                            E.wait_ge(s.h, v)

        def emit_ln_pre(src_fn):
            @on("vector")
            def _(vector):
                vector.wait_ge(sM.h, sM.n)
                vector.wait_ge(vM.h, vM.n)
                src = src_fn()
                for j in range(3):
                    vector.bn_stats(sStat[:, 6 * j:6 * j + 6],
                                    src[:, 512 * j:512 * (j + 1)])
                vM.inc(vector.bn_stats(sStat[:, 18:24], src[:, 1536:2048]), 1)
                vector.wait_ge(vM.h, vM.n)
                vM.inc(vector.bn_aggr(sMV[:], sStat[:]), 1)
                vector.wait_ge(vM.h, vM.n)
                vM.inc(vector.tensor_scalar(sT2[:], sMV[:, 1:2], EPS, None,
                                            op0=ALU.add), 1)
                vector.wait_ge(vM.h, vM.n)
                vM.inc(vector.reciprocal(sT0[:], sT2[:]), 1)
                vector.wait_ge(vM.h, vM.n)
                vM.inc(vector.tensor_scalar(sT1[:], sMV[:, 0:1], -1.0, None,
                                            op0=ALU.mult), 1)

            @on("scalar")
            def _(scalar):
                scalar.wait_ge(vM.h, vM.n)
                sM.inc(scalar.activation(sT0[:], sT0[:], AF.Sqrt), 1)

        def emit_wcast(wsrc, wdst, stg):
            stgRw = Ring(2)
            for ch in range(NCH):
                @on("sync")
                def _(sync, ch=ch):
                    stgRw.wait_free(sync)
                    dS.inc(sync.dma_start(out=stg[stgRw.np_ % 2][:],
                                          in_=wsrc[ch * 128:(ch + 1) * 128, :]), 16)
                    stgRw.produced(dS)

                @on("gpsimd")
                def _(g, ch=ch):
                    stgRw.wait_full(g)
                    pM.inc(g.tensor_copy(wdst[:, ch * D:(ch + 1) * D],
                                         stg[stgRw.nc_ % 2][:]), 1)
                    stgRw.consumed(pM)

        ctx0 = ExitStack()
        e0 = ctx0.enter_context
        sBrQ = e0(nc.sbuf_tensor("sBrQ", [1, D], F32))
        sBrK = e0(nc.sbuf_tensor("sBrK", [1, D], F32))
        sBrV = e0(nc.sbuf_tensor("sBrV", [1, D], F32))
        sBrO = e0(nc.sbuf_tensor("sBrO", [1, D], F32))
        sSb1r0 = e0(nc.sbuf_tensor("sSb1r0", [1, DF], F32))
        sSb1r1 = e0(nc.sbuf_tensor("sSb1r1", [1, DF], F32))
        sSb2r0 = e0(nc.sbuf_tensor("sSb2r0", [1, D], F32))
        sSb2r1 = e0(nc.sbuf_tensor("sSb2r1", [1, D], F32))
        sRb1F = e0(nc.sbuf_tensor("sRb1F", [32, DF], F32))
        sRb2F = e0(nc.sbuf_tensor("sRb2F", [32, D], F32))
        sRwF = e0(nc.sbuf_tensor("sRwF", [128, NCH * 32], F32))
        ctx0p = ExitStack()
        pF0 = ctx0p.enter_context(nc.psum_tensor("pF0", [128, 512], F32))
        pQ0 = ctx0p.enter_context(nc.psum_tensor("pQ0", [128, 512], F32))
        pQ1 = ctx0p.enter_context(nc.psum_tensor("pQ1", [128, 512], F32))
        pp0R = Ring(2)

        @on("sync")
        def _(sync):
            dS.inc(sync.dma_start(out=sIdent[:], in_=ident), 16)

        @on("scalar")
        def _(scalar):
            for dst, src in ((sBrQ, bq), (sBrK, bk), (sBrV, bv), (sBrO, bo),
                             (sSb1r0, sb1[0:1, :]), (sSb1r1, sb1[1:2, :]),
                             (sSb2r0, sb2[0:1, :]), (sSb2r1, sb2[1:2, :]),
                             (sRb1F, rb1), (sRb2F, rb2)):
                dC.inc(scalar.dma_start(out=dst[:], in_=src), 16)
            dC.inc(scalar.dma_start(
                out=sRwF[:], in_=rww.rearrange("(c p) e -> p c e", p=128)), 16)
            dC.inc(scalar.dma_start(
                out=sRwbT[:], in_=rwb.rearrange("o e -> e o")), 16)

        @on("vector")
        def _(vector):
            vector.memset(sOnesB[:], 1.0)
            vector.memset(sOnesF[:], 1.0)
            vector.wait_ge(dS.h, 16)
            vector.tensor_copy(sIdentF[:], sIdent[:])
            vM.inc(vector.memset(sOnesC[:], 1.0), 1)
            vector.wait_ge(dC.h, 16 * 12)
            vector.tensor_copy(sBvR[:], sBrV[:])
            vector.tensor_copy(sBoR[:], sBrO[:])
            vM.inc(vector.tensor_tensor(sSb2r0[:], sSb2r0[:], sSb2r1[:],
                                        op=ALU.add), 1)
            vector.wait_ge(vM.h, vM.n)
            vector.tensor_scalar(sSb2c[:], sSb2r0[:], 0.5, None, op0=ALU.mult)
            vector.tensor_copy(sRb2B[:], sRb2F[:])
            vM.inc(vector.tensor_copy(sRwB[:], sRwF[:]), 1)

        jobs = [(sBrQ, 16, 0), (sBrK, 16, 16), (sSb1r0, NHT, 32), (sSb1r1, NHT, 48)]

        @on("tensor")
        def _(tensor):
            tensor.wait_ge(dC.h, 16 * 12)
            tensor.wait_ge(vM.h, 1)
            first = True
            for srcT, n, base in jobs:
                for i in range(n):
                    tensor.matmul(pF0[:, base + i:base + i + 1],
                                  srcT[:, i * 128:(i + 1) * 128],
                                  sOnesF[:, 0:1], start=first, stop=False)
                    first = False
            tM.inc(tensor.matmul(pF0[0:1, 60:61], sOnesF[0:1, 0:1], sOnesF[0:1, 0:1],
                                 start=False, stop=True), 1)

        @on("vector")
        def _(vector):
            vector.wait_ge(tM.h, tM.n)
            for srcT, n, base in jobs:
                for i in range(n):
                    if base == 0:
                        vector.tensor_copy(sBqT[:, i:i + 1], pF0[:, i:i + 1])
                    elif base == 16:
                        vector.tensor_copy(sBkT[:, i:i + 1], pF0[:, 16 + i:17 + i])
                    elif base == 32:
                        vector.tensor_copy(sHsb1T[:, 2 * i:2 * i + 1],
                                           pF0[:, 32 + i:33 + i])
                    else:
                        vM.inc(vector.tensor_copy(sHsb1T[:, 2 * i + 1:2 * i + 2],
                                                  pF0[:, 48 + i:49 + i]), 1)
            vector.wait_ge(vM.h, vM.n)
            vM.inc(vector.tensor_scalar(sHsb1T[:], sHsb1T[:], 0.5, None,
                                        op0=ALU.mult), 1)

        for ht in range(NHT):
            @on("tensor")
            def _(tensor, ht=ht):
                pp0R.wait_free(tensor)
                tM.inc(tensor.transpose(
                    (pQ0 if pp0R.np_ % 2 == 0 else pQ1)[:, 0:32],
                    sRb1F[:, ht * 128:(ht + 1) * 128], sIdentF[0:32, 0:32]), 1)
                pp0R.produced(tM)

            @on("vector")
            def _(vector, ht=ht):
                pp0R.wait_full(vector)
                vM.inc(vector.tensor_copy(
                    sRb1T[:, ht * 32:(ht + 1) * 32],
                    (pQ0 if pp0R.nc_ % 2 == 0 else pQ1)[:, 0:32]), 1)
                pp0R.consumed(vM)

        barrier()
        ctx0p.close()
        ctx0.close()

        def emit_xpass(ntiles, src_of, sXl, sH1l, dst4_of, tile_done, psA, psB,
                       pre_evac_wait=None):
            xR = Ring(2)
            h1R = Ring(2)
            tpRl = Ring(2)
            for t in range(ntiles):
                @on("sync")
                def _(sync, t=t):
                    xR.wait_free(sync)
                    dS.inc(sync.dma_start(out=sXl[t % 2][:], in_=src_of(t)), 16)
                    xR.produced(dS)

                @on("vector")
                def _(vector, t=t):
                    xR.wait_full(vector)

                emit_ln_pre(lambda t=t: sXl[t % 2][:])

                @on("vector")
                def _(vector, t=t):
                    vector.wait_ge(sM.h, sM.n)
                    h1R.wait_free(vector)
                    vM.inc(vector.tensor_scalar(sH1l[t % 2][:], sXl[t % 2][:],
                                                sT1[:], sT0[:],
                                                op0=ALU.add, op1=ALU.mult), 1)
                    h1R.produced(vM)
                    xR.consumed(vM)

                for c4 in range(4):
                    @on("tensor")
                    def _(tensor, t=t, c4=c4):
                        if c4 == 0:
                            h1R.wait_full(tensor)
                        tpRl.wait_free(tensor)
                        pb = psA if tpRl.np_ % 2 == 0 else psB
                        ins = None
                        for j in range(4):
                            ins = tensor.transpose(
                                pb[:, j * 128:(j + 1) * 128],
                                sH1l[t % 2][:, (c4 * 4 + j) * 128:(c4 * 4 + j + 1) * 128],
                                sIdent[:])
                        tM.inc(ins, 1)
                        tpRl.produced(tM)

                    @on("vector")
                    def _(vector, t=t, c4=c4):
                        if c4 == 0 and pre_evac_wait is not None:
                            pre_evac_wait(vector, t)
                        tpRl.wait_full(vector)
                        pb = psA if tpRl.nc_ % 2 == 0 else psB
                        vM.inc(vector.tensor_copy(dst4_of(t, c4), pb[:]), 1)
                        tpRl.consumed(vM)
                        if c4 == 3:
                            h1R.consumed(vM)
                            tile_done(t)

        ctxQ = ExitStack()
        eQ = ctxQ.enter_context
        sWq = eQ(nc.sbuf_tensor("sWq", [128, NCH * D], BF16))
        stgQ = [eQ(nc.sbuf_tensor(f"stgQ{i}", [128, D], F32)) for i in range(2)]
        sXq = [eQ(nc.sbuf_tensor(f"sXq{i}", [128, D], F32)) for i in range(2)]
        sH1q = [eQ(nc.sbuf_tensor(f"sH1q{i}", [128, D], BF16)) for i in range(2)]
        sHbT = eQ(nc.sbuf_tensor("sHbT", [128, NCH * 128], BF16))
        sQbT = [eQ(nc.sbuf_tensor(f"sQbT{i}", [128, 2048], BF16)) for i in range(2)]
        ctxQp = ExitStack()
        pPl = [ctxQp.enter_context(nc.psum_tensor(f"pP{i}", [128, 512], F32))
               for i in range(4)]
        pTa = ctxQp.enter_context(nc.psum_tensor("pTa", [128, 512], BF16))
        pTb = ctxQp.enter_context(nc.psum_tensor("pTb", [128, 512], BF16))

        emit_wcast(wq, sWq, stgQ)

        @on("tensor")
        def _(tensor):
            tensor.wait_ge(pM.h, pM.n)

        ppR = Ring(4)
        hbtR = Ring(1)
        qbR = Ring(2)

        for hb in range(16):
            def dst4_of(t, c4):
                return sHbT[:, c4 * 512:(c4 + 1) * 512]

            def tile_done(t):
                hbtR.produced(vM)

            def pre_wait(vector, t):
                hbtR.wait_free(vector)

            emit_xpass(1, lambda t, hb=hb: xband[hb * 128:(hb + 1) * 128, :],
                       sXq, sH1q, dst4_of, tile_done, pTa, pTb, pre_wait)

            for c in range(NCH):
                @on("tensor")
                def _(tensor, hb=hb, c=c):
                    if c == 0:
                        hbtR.wait_full(tensor)
                        qbR.wait_free(tensor)
                    ppR.wait_free(tensor)
                    pp = pPl[ppR.np_ % 4]
                    ins = None
                    for ch in range(NCH):
                        ins = tensor.matmul(
                            pp[:, 0:128],
                            sWq[:, ch * D + c * 128: ch * D + (c + 1) * 128],
                            sHbT[:, ch * 128:(ch + 1) * 128],
                            start=(ch == 0), stop=(ch == NCH - 1))
                    tM.inc(ins, 1)
                    ppR.produced(tM)
                    if c == NCH - 1:
                        hbtR.consumed(tM)

                @on("vector")
                def _(vector, hb=hb, c=c):
                    ppR.wait_full(vector)
                    pp = pPl[ppR.nc_ % 4]
                    dst = sQbT[hb % 2][:].rearrange(
                        "p (k r c2) -> p k r c2", k=8, r=16)[:, :, :, c:c + 1]
                    src = pp[:, 0:128].rearrange("p (k r c2) -> p k r c2",
                                                 k=8, r=16, c2=1)
                    vM.inc(vector.tensor_scalar(dst, src, sBqT[:, c:c + 1],
                                                None, op0=ALU.add), 1)
                    ppR.consumed(vM)
                    if c == NCH - 1:
                        qbR.produced(vM)

            @on("sync")
            def _(sync, hb=hb):
                qbR.wait_full(sync)
                dS.inc(sync.dma_start(out=qscr[hb], in_=sQbT[hb % 2][:]), 16)
                qbR.consumed(dS)

        barrier()
        ctxQp.close()
        ctxQ.close()

        ctxK = ExitStack()
        eK = ctxK.enter_context
        sWk = eK(nc.sbuf_tensor("sWk", [128, NCH * D], BF16))
        stgK = [eK(nc.sbuf_tensor(f"stgK{i}", [128, D], F32)) for i in range(2)]
        sXk = [eK(nc.sbuf_tensor(f"sXk{i}", [128, D], F32)) for i in range(2)]
        sH1k = [eK(nc.sbuf_tensor(f"sH1k{i}", [128, D], BF16)) for i in range(2)]
        sH1T = eK(nc.sbuf_tensor("sH1T", [128, NCH * 512], BF16))
        sKst = eK(nc.sbuf_tensor("sKst", [128, H * 512], BF16))
        ctxKp = ExitStack()
        pPl = [ctxKp.enter_context(nc.psum_tensor(f"pP{i}k", [128, 512], F32))
               for i in range(4)]
        pTa = ctxKp.enter_context(nc.psum_tensor("pTak", [128, 512], BF16))
        pTb = ctxKp.enter_context(nc.psum_tensor("pTbk", [128, 512], BF16))

        emit_wcast(wk, sWk, stgK)

        @on("tensor")
        def _(tensor):
            tensor.wait_ge(pM.h, pM.n)

        ppR = Ring(4)
        h1aR = Ring(1)
        kstR = Ring(1)

        for sb in range(16):
            def dst4_of(t, c4):
                return sH1T[:].rearrange("p (c f) -> p c f", c=NCH)[
                    :, c4 * 4:(c4 + 1) * 4, t * 128:(t + 1) * 128]

            def tile_done(t):
                if t == 3:
                    h1aR.produced(vM)

            def pre_wait(vector, t):
                if t == 0:
                    h1aR.wait_free(vector)

            emit_xpass(4, lambda t, sb=sb: xb[(sb * 4 + t) * 128:(sb * 4 + t + 1) * 128, :],
                       sXk, sH1k, dst4_of, tile_done, pTa, pTb, pre_wait)

            for h in range(H):
                @on("tensor")
                def _(tensor, sb=sb, h=h):
                    if h == 0:
                        h1aR.wait_full(tensor)
                        kstR.wait_free(tensor)
                    ppR.wait_free(tensor)
                    pp = pPl[ppR.np_ % 4]
                    ins = None
                    for ch in range(NCH):
                        ins = tensor.matmul(
                            pp[:, 0:512],
                            sWk[:, ch * D + h * 128: ch * D + (h + 1) * 128],
                            sH1T[:, ch * 512:(ch + 1) * 512],
                            start=(ch == 0), stop=(ch == NCH - 1))
                    tM.inc(ins, 1)
                    ppR.produced(tM)
                    if h == H - 1:
                        h1aR.consumed(tM)

                @on("vector")
                def _(vector, sb=sb, h=h):
                    ppR.wait_full(vector)
                    pp = pPl[ppR.nc_ % 4]
                    vM.inc(vector.tensor_scalar(sKst[:, h * 512:(h + 1) * 512],
                                                pp[:, 0:512], sBkT[:, h:h + 1],
                                                None, op0=ALU.add), 1)
                    ppR.consumed(vM)
                    if h == H - 1:
                        kstR.produced(vM)

            @on("sync")
            def _(sync, sb=sb):
                kstR.wait_full(sync)
                dS.inc(sync.dma_start(
                    out=kscr[:, :, sb * 512:(sb + 1) * 512].rearrange("h p s -> p h s"),
                    in_=sKst[:].rearrange("p (h s) -> p h s", s=512)), 16)
                kstR.consumed(dS)

        barrier()
        ctxKp.close()
        ctxK.close()

        ctxV = ExitStack()
        eV = ctxV.enter_context
        sWv = eV(nc.sbuf_tensor("sWv", [128, NCH * D], BF16))
        stgV = [eV(nc.sbuf_tensor(f"stgV{i}", [128, D], F32)) for i in range(2)]
        sXv = [eV(nc.sbuf_tensor(f"sXv{i}", [128, D], F32)) for i in range(2)]
        sH1v = [eV(nc.sbuf_tensor(f"sH1v{i}", [128, D], BF16)) for i in range(2)]
        sH1Tv = eV(nc.sbuf_tensor("sH1Tv", [128, NCH * 512], BF16))
        sVst = [eV(nc.sbuf_tensor(f"sVst{i}", [128, D], BF16)) for i in range(2)]
        ctxVp = ExitStack()
        pPl = [ctxVp.enter_context(nc.psum_tensor(f"pP{i}v", [128, 512], F32))
               for i in range(4)]
        pTa = ctxVp.enter_context(nc.psum_tensor("pTav", [128, 512], BF16))
        pTb = ctxVp.enter_context(nc.psum_tensor("pTbv", [128, 512], BF16))

        emit_wcast(wv, sWv, stgV)

        @on("tensor")
        def _(tensor):
            tensor.wait_ge(pM.h, pM.n)

        ppR = Ring(4)
        h1aR = Ring(1)
        vstR = Ring(2)

        for sb in range(16):
            def dst4_of(t, c4):
                return sH1Tv[:].rearrange("p (c f) -> p c f", c=NCH)[
                    :, c4 * 4:(c4 + 1) * 4, t * 128:(t + 1) * 128]

            def tile_done(t):
                if t == 3:
                    h1aR.produced(vM)

            def pre_wait(vector, t):
                if t == 0:
                    h1aR.wait_free(vector)

            emit_xpass(4, lambda t, sb=sb: xb[(sb * 4 + t) * 128:(sb * 4 + t + 1) * 128, :],
                       sXv, sH1v, dst4_of, tile_done, pTa, pTb, pre_wait)

            for tt in range(4):
                for nb in range(4):
                    @on("tensor")
                    def _(tensor, sb=sb, tt=tt, nb=nb):
                        if tt == 0 and nb == 0:
                            h1aR.wait_full(tensor)
                        if nb == 0:
                            vstR.wait_free(tensor)
                        ppR.wait_free(tensor)
                        pp = pPl[ppR.np_ % 4]
                        tensor.matmul(pp[:, 0:512], sOnesB[:],
                                      sBvR[:, nb * 512:(nb + 1) * 512],
                                      start=True, stop=False)
                        ins = None
                        for ch in range(NCH):
                            ins = tensor.matmul(
                                pp[:, 0:512],
                                sH1Tv[:, ch * 512 + tt * 128: ch * 512 + (tt + 1) * 128],
                                sWv[:, ch * D + nb * 512: ch * D + (nb + 1) * 512],
                                start=False, stop=(ch == NCH - 1))
                        tM.inc(ins, 1)
                        ppR.produced(tM)
                        if tt == 3 and nb == 3:
                            h1aR.consumed(tM)

                    @on("vector")
                    def _(vector, sb=sb, tt=tt, nb=nb):
                        ppR.wait_full(vector)
                        pp = pPl[ppR.nc_ % 4]
                        vM.inc(vector.tensor_copy(
                            sVst[tt % 2][:, nb * 512:(nb + 1) * 512],
                            pp[:, 0:512]), 1)
                        ppR.consumed(vM)
                        if nb == 3:
                            vstR.produced(vM)

                @on("sync")
                def _(sync, sb=sb, tt=tt):
                    vstR.wait_full(sync)
                    st = sb * 4 + tt
                    dS.inc(sync.dma_start(
                        out=vscr[:, :, st * 128:(st + 1) * 128].rearrange("h p d -> p h d"),
                        in_=sVst[tt % 2][:].rearrange("p (h d) -> p h d", d=128)), 16)
                    vstR.consumed(dS)

        if DEBUG >= 1:
            barrier()

            @on("sync")
            def _(sync):
                for hh in range(H):
                    dS.inc(sync.dma_start(out=dbg_k[hh], in_=kscr[hh]), 16)
                    dS.inc(sync.dma_start(out=dbg_v[hh], in_=vscr[hh]), 16)
                    dS.inc(sync.dma_start(out=dbg_q[hh], in_=qscr[hh]), 16)
                sync.wait_ge(dS.h, dS.n)

        barrier()
        ctxVp.close()
        ctxV.close()

        ctxB = ExitStack()
        eB = ctxB.enter_context
        sK = [eB(nc.sbuf_tensor(f"sK{i}", [128, S], BF16)) for i in range(2)]
        sV = [eB(nc.sbuf_tensor(f"sV{i}", [128, S], BF16)) for i in range(2)]
        sQ = [eB(nc.sbuf_tensor(f"sQ{i}", [128, 2048], BF16)) for i in range(2)]
        sMk = [eB(nc.sbuf_tensor(f"sMk{i}", [128, 2048], BF16)) for i in range(2)]
        sPt = [eB(nc.sbuf_tensor(f"sPt{i}", [128, 512], BF16)) for i in range(3)]
        sQsq = eB(nc.sbuf_tensor("sQsq", [128, 2048], BF16))
        sRL = eB(nc.sbuf_tensor("sRL", [1, 256], F32))
        sBcB = eB(nc.sbuf_tensor("sBcB", [128, 256], BF16))
        sAo = [eB(nc.sbuf_tensor(f"sAo{i}", [128, 256], BF16)) for i in range(2)]
        ctxB1 = ExitStack()
        pS = [ctxB1.enter_context(nc.psum_tensor(f"pS{i}", [128, 512], F32))
              for i in range(3)]
        pOut = ctxB1.enter_context(nc.psum_tensor("pOut", [128, 256], F32))
        pL = ctxB1.enter_context(nc.psum_tensor("pL", [1, 256], F32))
        pBc = ctxB1.enter_context(nc.psum_tensor("pBc", [128, 256], F32))
        pRow = ctxB1.enter_context(nc.psum_tensor("pRow", [1, 512], F32))

        kR = Ring(2)
        vR = Ring(2)
        qR = Ring(2)
        mkR = Ring(2)
        ptR = Ring(3)
        psR = Ring(3)
        aoR = Ring(2)

        for h in range(H):
            @on("sync")
            def _(sync, h=h):
                for ring, dst, src in ((kR, sK, kscr), (vR, sV, vscr), (qR, sQ, qscr)):
                    ring.wait_free(sync)
                    dS.inc(sync.dma_start(out=dst[h % 2][:], in_=src[h]), 16)
                    ring.produced(dS)

            @on("scalar")
            def _(scalar, h=h):
                qR.wait_full(scalar)
                scalar.wait_ge(tM.h, tM.n)
                sM.inc(scalar.activation(sQsq[:], sQ[h % 2][:], AF.Square), 1)

            @on("tensor")
            def _(tensor, h=h):
                tensor.wait_ge(sM.h, sM.n)
                for j in range(4):
                    tensor.wait_ge(vM.h, vM.n + j)
                    tM.inc(tensor.matmul(pRow[:, 0:512], sOnesC[:],
                                         sQsq[:, j * 512:(j + 1) * 512],
                                         start=True, stop=True), 1)

            @on("vector")
            def _(vector, h=h):
                for j in range(4):
                    vector.wait_ge(tM.h, tM.n - 3 + j)
                    vM.inc(vector.reduce_max(sMx[:, j:j + 1], pRow[:, 0:512],
                                             axis=AX.X), 1)

            for j4 in range(4):
                @on("scalar")
                def _(scalar, h=h, j4=j4):
                    if j4 == 0:
                        kR.wait_full(scalar)
                    scalar.wait_ge(tM.h, tM.n)
                    sM.inc(scalar.activation(
                        sQsq[:], sK[h % 2][:, j4 * 2048:(j4 + 1) * 2048],
                        AF.Square), 1)

                @on("tensor")
                def _(tensor, h=h, j4=j4):
                    tensor.wait_ge(sM.h, sM.n)
                    for j in range(4):
                        tensor.wait_ge(vM.h, vM.n + j)
                        tM.inc(tensor.matmul(pRow[:, 0:512], sOnesC[:],
                                             sQsq[:, j * 512:(j + 1) * 512],
                                             start=True, stop=True), 1)

                @on("vector")
                def _(vector, h=h, j4=j4):
                    for j in range(4):
                        vector.wait_ge(tM.h, tM.n - 3 + j)
                        vM.inc(vector.reduce_max(
                            sMx[:, 4 + j4 * 4 + j:5 + j4 * 4 + j],
                            pRow[:, 0:512], axis=AX.X), 1)

            @on("vector")
            def _(vector, h=h):
                vector.wait_ge(vM.h, vM.n)
                vector.reduce_max(sSc0[:], sMx[:, 0:4], axis=AX.X)
                vM.inc(vector.reduce_max(sSc1[:], sMx[:, 4:20], axis=AX.X), 1)
                vector.wait_ge(vM.h, vM.n)
                vM.inc(vector.tensor_tensor(sSc0[:], sSc0[:], sSc1[:],
                                            op=ALU.mult), 1)

            @on("scalar")
            def _(scalar, h=h):
                scalar.wait_ge(vM.h, vM.n)
                sM.inc(scalar.activation(sSc0[:], sSc0[:], AF.Sqrt), 1)

            @on("vector")
            def _(vector, h=h):
                vector.wait_ge(sM.h, sM.n)
                vM.inc(vector.tensor_scalar(sSc0[:], sSc0[:], -SCALE, None,
                                            op0=ALU.mult), 1)

            @on("tensor")
            def _(tensor, h=h):
                tensor.wait_ge(vM.h, vM.n)
                tM.inc(tensor.matmul(pBc[:, 0:1], sOnesF[:], sSc0[:],
                                     start=True, stop=True), 1)

            @on("vector")
            def _(vector, h=h):
                vector.wait_ge(tM.h, tM.n)
                vM.inc(vector.tensor_copy(sBias[:], pBc[:, 0:1]), 1)

            for k in range(NBLK):
                ngr = 4 * k + 4

                @on("sync")
                def _(sync, h=h, k=k):
                    mkR.wait_free(sync)
                    dS.inc(sync.dma_start(out=sMk[mkR.np_ % 2][:],
                                          in_=masksI[k]), 16)
                    mkR.produced(dS)

                for g in range(ngr):
                    masked = g >= ngr - 4

                    @on("tensor")
                    def _(tensor, h=h, k=k, g=g):
                        if g == 0:
                            vR.wait_full(tensor)
                        psR.wait_free(tensor)
                        ps = pS[psR.np_ % 3]
                        ins = None
                        for i in range(2):
                            c = 2 * g + i
                            ins = tensor.matmul(
                                ps[:, i * 256:(i + 1) * 256],
                                sK[h % 2][:, c * 128:(c + 1) * 128],
                                sQ[h % 2][:, k * 256:(k + 1) * 256],
                                start=True, stop=True)
                        tM.inc(ins, 1)
                        psR.produced(tM)

                    @on("scalar")
                    def _(scalar, h=h, k=k, g=g, masked=masked):
                        psR.wait_full(scalar)
                        ptR.wait_free(scalar)
                        ps = pS[psR.nc_ % 3]
                        pt = sPt[ptR.np_ % 3]
                        sM.inc(scalar.activation(pt[:], ps[:], AF.Exp,
                                                 bias=sBias[:], scale=SCALE), 1)
                        psR.consumed(sM)
                        if not masked:
                            ptR.produced(sM)

                    if masked:
                        @on("vector")
                        def _(vector, h=h, k=k, g=g):
                            vector.wait_ge(sM.h, sM.n)
                            if g == 4 * k:
                                mkR.wait_full(vector)
                            pt = sPt[ptR.np_ % 3]
                            mof = (g - 4 * k) * 512
                            vM.inc(vector.tensor_tensor(
                                pt[:], pt[:],
                                sMk[mkR.nc_ % 2][:, mof:mof + 512],
                                op=ALU.mult), 1)
                            ptR.produced(vM)
                            if g == 4 * k + 3:
                                mkR.consumed(vM)

                    @on("tensor")
                    def _(tensor, h=h, k=k, g=g, ngr=ngr):
                        if g == 0:
                            tensor.wait_ge(vM.h, vM.n)
                        ptR.wait_full(tensor)
                        pt = sPt[ptR.nc_ % 3]
                        first = g == 0
                        ins = None
                        for i in range(2):
                            c = 2 * g + i
                            tensor.matmul(pL[:], sOnesC[:],
                                          pt[:, i * 256:(i + 1) * 256],
                                          start=(first and i == 0),
                                          stop=(g == ngr - 1 and i == 1))
                            ins = tensor.matmul(
                                pOut[:], sV[h % 2][:, c * 128:(c + 1) * 128],
                                pt[:, i * 256:(i + 1) * 256],
                                start=(first and i == 0),
                                stop=(g == ngr - 1 and i == 1))
                        tM.inc(ins, 1)
                        ptR.consumed(tM)
                        if g == ngr - 1 and k == NBLK - 1:
                            kR.consumed(tM)
                            qR.consumed(tM)
                            vR.consumed(tM)

                @on("vector")
                def _(vector, h=h, k=k):
                    s, v = ptR.cv[-1]
                    vector.wait_ge(s.h, v)
                    vM.inc(vector.reciprocal(sRL[:], pL[:]), 1)

                @on("tensor")
                def _(tensor, h=h, k=k):
                    tensor.wait_ge(vM.h, vM.n)
                    tM.inc(tensor.matmul(pBc[:], sOnesF[:], sRL[:],
                                         start=True, stop=True), 1)

                @on("scalar")
                def _(scalar, h=h, k=k):
                    scalar.wait_ge(tM.h, tM.n)
                    sM.inc(scalar.copy(sBcB[:], pBc[:]), 1)

                @on("vector")
                def _(vector, h=h, k=k):
                    vector.wait_ge(sM.h, sM.n)
                    aoR.wait_free(vector)
                    vM.inc(vector.tensor_tensor(sAo[aoR.np_ % 2][:],
                                                pOut[:], sBcB[:], op=ALU.mult), 1)
                    aoR.produced(vM)

                @on("sync")
                def _(sync, h=h, k=k):
                    aoR.wait_full(sync)
                    dS.inc(sync.dma_start(out=ascr[h][:, k * 256:(k + 1) * 256],
                                          in_=sAo[aoR.nc_ % 2][:]), 16)
                    aoR.consumed(dS)

        if DEBUG >= 2:
            barrier()

            @on("sync")
            def _(sync):
                for hh in range(H):
                    dS.inc(sync.dma_start(out=dbg_at[hh], in_=ascr[hh]), 16)
                sync.wait_ge(dS.h, dS.n)

        barrier()
        ctxB1.close()
        ctxB.close()

        ctxP2 = ExitStack()
        e2 = ctxP2.enter_context
        sWo = e2(nc.sbuf_tensor("sWo", [128, NCH * D], BF16))
        stgB = [e2(nc.sbuf_tensor(f"stgB{i}", [128, D], F32)) for i in range(2)]
        sXm = [e2(nc.sbuf_tensor(f"sXm{i}", [128, D], F32)) for i in range(2)]
        sX1 = [e2(nc.sbuf_tensor(f"sX1{i}", [128, D], F32)) for i in range(2)]
        sH2 = e2(nc.sbuf_tensor("sH2", [128, D], BF16))
        sH2T = e2(nc.sbuf_tensor("sH2T", [128, NCH * 128], BF16))
        sAt = [e2(nc.sbuf_tensor(f"sAt{i}", [128, 128], BF16)) for i in range(16)]
        sPj = [e2(nc.sbuf_tensor(f"sPj{i}", [128, D], F32)) for i in range(2)]
        sLgT = e2(nc.sbuf_tensor("sLgT", [32, 128], F32))
        sLg = e2(nc.sbuf_tensor("sLg", [128, 32], F32))
        sU = e2(nc.sbuf_tensor("sU", [128, 32], F32))
        sUc = e2(nc.sbuf_tensor("sUc", [128, 32], F32))
        sMsk = e2(nc.sbuf_tensor("sMsk", [128, 32], F32))
        sM6 = e2(nc.sbuf_tensor("sM6", [128, 6], F32))
        sSum = e2(nc.sbuf_tensor("sSum", [128, 1], F32))
        sGt = e2(nc.sbuf_tensor("sGt", [128, 32], F32))
        ctxP2p = ExitStack()
        pPj = ctxP2p.enter_context(nc.psum_tensor("pPj", [128, 2048], F32))
        pTc = ctxP2p.enter_context(nc.psum_tensor("pTc", [128, 512], BF16))
        pTd = ctxP2p.enter_context(nc.psum_tensor("pTd", [128, 512], BF16))
        pR32 = ctxP2p.enter_context(nc.psum_tensor("pR32", [32, 128], F32))
        pTf = ctxP2p.enter_context(nc.psum_tensor("pTf", [128, 32], F32))

        emit_wcast(wo, sWo, stgB)

        @on("tensor")
        def _(tensor):
            tensor.wait_ge(pM.h, pM.n)

        xmR = Ring(2)
        atR = Ring(16)
        x1R = Ring(2)
        pjR = Ring(1)
        pjsR = Ring(2)
        h2R = Ring(1)
        h2tR = Ring(1)
        tp2R = Ring(2)
        h2dma = [0]

        for ti in range(16):
            @on("sync")
            def _(sync, ti=ti):
                xmR.wait_free(sync)
                dS.inc(sync.dma_start(out=sXm[ti % 2][:],
                                      in_=xmine[ti * 128:(ti + 1) * 128, :]), 16)
                xmR.produced(dS)
                for ch in range(NCH):
                    atR.wait_free(sync)
                    dS.inc(sync.dma_start(
                        out=sAt[atR.np_ % 16][:],
                        in_=ascr[ch][:, ti * 128:(ti + 1) * 128]), 16)
                    atR.produced(dS)

            @on("tensor")
            def _(tensor, ti=ti):
                pjR.wait_free(tensor)
                for db in range(4):
                    tensor.matmul(pPj[:, db * 512:(db + 1) * 512], sOnesB[:],
                                  sBoR[:, db * 512:(db + 1) * 512],
                                  start=True, stop=False)
                ins = None
                for ch in range(NCH):
                    atR.wait_full(tensor)
                    for db in range(4):
                        ins = tensor.matmul(
                            pPj[:, db * 512:(db + 1) * 512], sAt[atR.nc_ % 16][:],
                            sWo[:, ch * D + db * 512: ch * D + (db + 1) * 512],
                            start=False, stop=(ch == NCH - 1))
                    tM.inc(ins, 1)
                    atR.consumed(tM)
                pjR.produced(tM)

            @on("vector")
            def _(vector, ti=ti):
                pjR.wait_full(vector)
                pjsR.wait_free(vector)
                vector.wait_ge(vM.h, vM.n)
                vM.inc(vector.tensor_copy(sPj[pjsR.np_ % 2][:], pPj[:]), 1)
                pjR.consumed(vM)
                pjsR.produced(vM)

            emit_ln_pre(lambda ti=ti: sPj[(pjsR.np_ - 1) % 2][:])

            @on("vector")
            def _(vector, ti=ti):
                vector.wait_ge(sM.h, sM.n)
                xmR.wait_full(vector)
                x1R.wait_free(vector)
                s1 = sX1[ti % 2]
                vM.inc(vector.tensor_scalar(s1[:], sPj[(pjsR.np_ - 1) % 2][:],
                                            sT1[:], sT0[:],
                                            op0=ALU.add, op1=ALU.mult), 1)
                vector.wait_ge(vM.h, vM.n)
                vM.inc(vector.tensor_tensor(s1[:], s1[:], sXm[ti % 2][:],
                                            op=ALU.add), 1)
                xmR.consumed(vM)
                pjsR.consumed(vM)
                x1R.produced(vM)

            emit_ln_pre(lambda ti=ti: sX1[ti % 2][:])

            @on("vector")
            def _(vector, ti=ti):
                vector.wait_ge(sM.h, sM.n)
                h2R.wait_free(vector)
                vM.inc(vector.tensor_scalar(sH2[:], sX1[ti % 2][:],
                                            sT1[:], sT0[:],
                                            op0=ALU.add, op1=ALU.mult), 1)
                h2R.produced(vM)

            @on("sync")
            def _(sync, ti=ti):
                x1R.wait_full(sync)
                dS.inc(sync.dma_start(out=x1s[ti], in_=sX1[ti % 2][:]), 16)
                x1R.consumed(dS)

            for ch in range(NCH):
                @on("tensor")
                def _(tensor, ti=ti, ch=ch):
                    if ch == 0:
                        h2R.wait_full(tensor)
                        h2tR.wait_free(tensor)
                    tp2R.wait_free(tensor)
                    ins = tensor.transpose(
                        (pTc if tp2R.np_ % 2 == 0 else pTd)[:, 0:128],
                        sH2[:, ch * 128:(ch + 1) * 128], sIdent[:])
                    tM.inc(ins, 1)
                    tp2R.produced(tM)
                    if ch == NCH - 1:
                        h2R.consumed(tM)

                @on("vector")
                def _(vector, ti=ti, ch=ch):
                    if ch == 0:
                        vector.wait_ge(dS.h, h2dma[0])
                    tp2R.wait_full(vector)
                    vM.inc(vector.tensor_copy(
                        sH2T[:, ch * 128:(ch + 1) * 128],
                        (pTc if tp2R.nc_ % 2 == 0 else pTd)[:, 0:128]), 1)
                    tp2R.consumed(vM)
                    if ch == NCH - 1:
                        h2tR.produced(vM)

            @on("sync")
            def _(sync, ti=ti):
                h2tR.wait_full(sync)
                dS.inc(sync.dma_start(
                    out=h2ts[:, :, ti * 128:(ti + 1) * 128].rearrange("c p t -> p c t"),
                    in_=sH2T[:].rearrange("p (c t) -> p c t", t=128)), 16)
            h2dma[0] = dS.n

            @on("tensor")
            def _(tensor, ti=ti):
                ins = None
                for ch in range(NCH):
                    ins = tensor.matmul(pR32[:], sRwB[:, ch * 32:(ch + 1) * 32],
                                        sH2T[:, ch * 128:(ch + 1) * 128],
                                        start=(ch == 0), stop=(ch == NCH - 1))
                tM.inc(ins, 1)
                h2tR.consumed(tM)

            @on("vector")
            def _(vector, ti=ti):
                vector.wait_ge(tM.h, tM.n)
                vM.inc(vector.tensor_scalar(sLgT[:], pR32[:], sRwbT[:], None,
                                            op0=ALU.add), 1)

            @on("tensor")
            def _(tensor, ti=ti):
                tensor.wait_ge(vM.h, vM.n)
                tM.inc(tensor.transpose(pTf[:], sLgT[:], sIdentF[0:32, 0:32]), 1)

            @on("vector")
            def _(vector, ti=ti):
                vector.wait_ge(tM.h, tM.n)
                vM.inc(vector.tensor_copy(sLg[:], pTf[:]), 1)
                vector.wait_ge(vM.h, vM.n)
                vM.inc(vector.reduce_max(sT1[:], sLg[:], axis=AX.X), 1)
                vector.wait_ge(vM.h, vM.n)
                vM.inc(vector.tensor_scalar(sT1[:], sT1[:], -1.0, None,
                                            op0=ALU.mult), 1)

            @on("scalar")
            def _(scalar, ti=ti):
                scalar.wait_ge(vM.h, vM.n)
                sM.inc(scalar.activation(sU[:], sLg[:], AF.Exp, bias=sT1[:]), 1)

            @on("vector")
            def _(vector, ti=ti):
                vector.wait_ge(sM.h, sM.n)

                def step(ins):
                    vM.inc(ins, 1)
                    vector.wait_ge(vM.h, vM.n)
                vector.memset(sSum[:], 0.0)
                step(vector.tensor_copy(sUc[:], sU[:]))
                for i in range(6):
                    step(vector.reduce_max(sM6[:, i:i + 1], sUc[:], axis=AX.X))
                    step(vector.tensor_tensor(sSum[:], sSum[:], sM6[:, i:i + 1],
                                              op=ALU.add))
                    if i < 5:
                        step(vector.tensor_scalar(sMsk[:], sUc[:],
                                                  sM6[:, i:i + 1],
                                                  None, op0=ALU.is_lt))
                        step(vector.tensor_tensor(sUc[:], sUc[:], sMsk[:],
                                                  op=ALU.mult))
                step(vector.tensor_scalar(sMsk[:], sU[:], sM6[:, 5:6], None,
                                          op0=ALU.is_ge))
                step(vector.reciprocal(sSum[:], sSum[:]))
                step(vector.tensor_scalar(sGt[:], sU[:], sSum[:], None,
                                          op0=ALU.mult))
                vM.inc(vector.tensor_tensor(sGt[:], sGt[:], sMsk[:],
                                            op=ALU.mult), 1)

            @on("tensor")
            def _(tensor, ti=ti):
                tensor.wait_ge(vM.h, vM.n)
                tM.inc(tensor.transpose(pR32[:], sGt[:], sIdentF[:]), 1)

            @on("vector")
            def _(vector, ti=ti):
                vector.wait_ge(tM.h, tM.n)
                vM.inc(vector.tensor_copy(sGatesT[:, ti * 128:(ti + 1) * 128],
                                          pR32[:]), 1)

        if DEBUG >= 3:
            barrier()

            @on("sync")
            def _(sync):
                for tt in range(16):
                    dS.inc(sync.dma_start(out=dbg_x1[tt], in_=x1s[tt]), 16)
                for cc in range(NCH):
                    dS.inc(sync.dma_start(out=dbg_h2t[cc], in_=h2ts[cc]), 16)
                sync.wait_ge(dS.h, dS.n)

            @on("scalar")
            def _(scalar):
                scalar.wait_ge(vM.h, vM.n)
                dC.inc(scalar.dma_start(out=dbg_g, in_=sGatesT[:]), 16)
                scalar.wait_ge(dC.h, dC.n)

        barrier()
        ctxP2p.close()
        ctxP2.close()

        ctxC = ExitStack()
        eC = ctxC.enter_context
        sAcc = eC(nc.sbuf_tensor("sAcc", [128, 4 * D], F32))
        sG = eC(nc.sbuf_tensor("sG", [128, NHT * 512], BF16))
        sW2 = eC(nc.sbuf_tensor("sW2", [128, NHT * D], BF16))
        sW1g = [eC(nc.sbuf_tensor(f"sW1g{i}", [128, 2 * D], BF16)) for i in range(2)]
        stgC = [eC(nc.sbuf_tensor(f"stgC{i}", [128, D], F32)) for i in range(2)]
        sH2b = eC(nc.sbuf_tensor("sH2b", [128, NCH * 512], BF16))
        sGb = [eC(nc.sbuf_tensor(f"sGb{i}", [128, 512], BF16)) for i in range(2)]
        sRelu = [eC(nc.sbuf_tensor(f"sRelu{i}", [128, 512], BF16)) for i in range(2)]
        sX1t = eC(nc.sbuf_tensor("sX1t", [128, D], F32))
        sX2 = eC(nc.sbuf_tensor("sX2", [128, D], F32))

        pA10 = ctxC.enter_context(nc.psum_tensor("pA10", [128, 512], F32))
        pA11 = ctxC.enter_context(nc.psum_tensor("pA11", [128, 512], F32))
        pA20 = ctxC.enter_context(nc.psum_tensor("pA20", [128, 512], F32))
        pA21 = ctxC.enter_context(nc.psum_tensor("pA21", [128, 512], F32))
        pG = ctxC.enter_context(nc.psum_tensor("pG", [128, 512], F32))

        w1R = Ring(2)
        w2R = Ring(1)
        stcR = Ring(2)
        h2bR = Ring(1)
        gbR = Ring(2)
        grR = Ring(1)
        pa1R = Ring(2)
        pa2R = Ring(2)
        reR = Ring(2)
        x1tR = Ring(1)
        x2R = Ring(1)

        NEXP = 34
        w1save_ev = []
        w2sv = []

        def wsrc1(ei):
            return sw1[ei] if ei < 2 else rw1[ei - 2]

        def wsrc2(ei):
            return sw2[ei] if ei < 2 else rw2[ei - 2]

        for qp in range(4):
            for ei in range(NEXP):
                routed = ei >= 2

                @on("scalar")
                def _(scalar, qp=qp, ei=ei, routed=routed):
                    if routed:
                        grR.wait_free(scalar)
                        dC.inc(scalar.dma_start(
                            out=sGRow[:],
                            in_=sGatesT[ei - 2:ei - 1, qp * 512:(qp + 1) * 512]), 16)
                        grR.produced(dC)
                    h2bR.wait_free(scalar)
                    dC.inc(scalar.dma_start(
                        out=sH2b[:],
                        in_=h2ts[:, :, qp * 512:(qp + 1) * 512]
                        .rearrange("c p t -> p c t")), 16)
                    h2bR.produced(dC)

                GRPS = [(0, 2), (2, 2), (4, 2), (6, 2), (8, 2), (10, 1)]

                def emit_w1_group(gix, qp=qp, ei=ei):
                    ht0, gn = GRPS[gix]
                    if qp == 0:
                        @on("sync")
                        def _(sync, ht0=ht0, gn=gn, ei=ei):
                            for j in range(gn):
                                stcR.wait_free(sync)
                                dS.inc(sync.dma_start(
                                    out=stgC[stcR.np_ % 2][:, 0:NCH * 128],
                                    in_=wsrc1(ei)[:, (ht0 + j) * 128:(ht0 + j + 1) * 128]
                                    .rearrange("(c p) f -> p c f", p=128)), 16)
                                stcR.produced(dS)

                        @on("vector")
                        def _(vector, gn=gn):
                            vector.wait_ge(vM.h, vM.n)
                            for j in range(gn):
                                stcR.wait_full(vector)
                                if j == 0:
                                    w1R.wait_free(vector)
                                    i = w1R.np_
                                    if i >= 2 and w1save_ev[i - 2] is not None:
                                        vector.wait_ge(dC.h, w1save_ev[i - 2])
                                vM.inc(vector.tensor_copy(
                                    sW1g[w1R.np_ % 2][:, j * D:(j + 1) * D],
                                    stgC[stcR.nc_ % 2][:, 0:NCH * 128]), 1)
                                stcR.consumed(vM)
                                if j == gn - 1:
                                    w1R.produced(vM)

                        @on("scalar")
                        def _(scalar, gix=gix, ei=ei):
                            s, v = w1R.pv[-1]
                            scalar.wait_ge(s.h, v)
                            dC.inc(scalar.dma_start(
                                out=wscr1[ei][gix],
                                in_=sW1g[(w1R.np_ - 1) % 2][:, 0:4096]), 16)
                            w1save_ev.append(dC.n)
                    else:
                        @on("sync")
                        def _(sync, gix=gix, ei=ei):
                            w1R.wait_free(sync)
                            i = w1R.np_
                            if i >= 2 and w1save_ev[i - 2] is not None:
                                sync.wait_ge(dC.h, w1save_ev[i - 2])
                            dS.inc(sync.dma_start(
                                out=sW1g[w1R.np_ % 2][:, 0:4096],
                                in_=wscr1[ei][gix]), 16)
                            w1R.produced(dS)
                            w1save_ev.append(None)

                def emit_w2(qp=qp, ei=ei):
                    if qp == 0:
                        for hc in range(NHT):
                            @on("sync")
                            def _(sync, hc=hc, ei=ei):
                                stcR.wait_free(sync)
                                dS.inc(sync.dma_start(
                                    out=stgC[stcR.np_ % 2][:],
                                    in_=wsrc2(ei)[hc * 128:(hc + 1) * 128, :]), 16)
                                stcR.produced(dS)

                            @on("vector")
                            def _(vector, hc=hc):
                                stcR.wait_full(vector)
                                if hc == 0:
                                    w2R.wait_free(vector)
                                    if w2sv:
                                        vector.wait_ge(dC.h, w2sv[-1])
                                vector.wait_ge(vM.h, vM.n)
                                vM.inc(vector.tensor_copy(
                                    sW2[:, hc * D:(hc + 1) * D],
                                    stgC[stcR.nc_ % 2][:]), 1)
                                stcR.consumed(vM)
                                if hc == NHT - 1:
                                    w2R.produced(vM)

                        @on("scalar")
                        def _(scalar, ei=ei):
                            s, v = w2R.pv[-1]
                            scalar.wait_ge(s.h, v)
                            dC.inc(scalar.dma_start(out=wscr2[ei], in_=sW2[:]), 16)
                            w2sv.append(dC.n)
                    else:
                        @on("sync")
                        def _(sync, ei=ei):
                            w2R.wait_free(sync)
                            if w2sv:
                                sync.wait_ge(dC.h, w2sv[-1])
                            dS.inc(sync.dma_start(out=sW2[:], in_=wscr2[ei]), 16)
                            w2R.produced(dS)

                if routed:
                    @on("tensor")
                    def _(tensor, qp=qp, ei=ei):
                        grR.wait_full(tensor)
                        gbR.wait_free(tensor)
                        tM.inc(tensor.matmul(pG[:], sOnesB[:], sGRow[:],
                                             start=True, stop=True), 1)

                    @on("scalar")
                    def _(scalar, qp=qp, ei=ei):
                        scalar.wait_ge(tM.h, tM.n)
                        sM.inc(scalar.copy(sGb[gbR.np_ % 2][:], pG[:]), 1)
                        gbR.produced(sM)
                        grR.consumed(sM)

                for ht in range(NHT):
                    gix, jj = divmod(ht, 2)
                    last_of_grp = (ht % 2 == 1) or ht == NHT - 1
                    if jj == 0:
                        emit_w1_group(gix)

                    @on("tensor")
                    def _(tensor, qp=qp, ei=ei, ht=ht, gix=gix, jj=jj,
                          last_of_grp=last_of_grp):
                        if ht == 0:
                            h2bR.wait_full(tensor)
                        if jj == 0:
                            w1R.wait_full(tensor)
                        pa1R.wait_free(tensor)
                        pa = pA10 if pa1R.np_ % 2 == 0 else pA11
                        w1s = sW1g[w1R.nc_ % 2]
                        ins = None
                        for ch in range(NCH):
                            ins = tensor.matmul(
                                pa[:], w1s[:, jj * D + ch * 128: jj * D + (ch + 1) * 128],
                                sH2b[:, ch * 512:(ch + 1) * 512],
                                start=(ch == 0), stop=(ch == NCH - 1))
                        tM.inc(ins, 1)
                        if last_of_grp:
                            w1R.consumed(tM)
                        pa1R.produced(tM)
                        if ht == NHT - 1:
                            h2bR.consumed(tM)

                    @on("scalar")
                    def _(scalar, qp=qp, ei=ei, ht=ht, routed=routed):
                        pa1R.wait_full(scalar)
                        pa = pA10 if pa1R.nc_ % 2 == 0 else pA11
                        if routed:
                            reR.wait_free(scalar)
                            sM.inc(scalar.activation(
                                sRelu[reR.np_ % 2][:], pa[:], AF.Relu,
                                bias=sRb1T[:, ht * 32 + ei - 2: ht * 32 + ei - 1]), 1)
                            pa1R.consumed(sM)
                            reR.produced(sM)
                        else:
                            sM.inc(scalar.activation(
                                sG[:, ht * 512:(ht + 1) * 512],
                                pa[:], AF.Relu, scale=0.5,
                                bias=sHsb1T[:, ht * 2 + ei: ht * 2 + ei + 1]), 1)
                            pa1R.consumed(sM)

                    if routed:
                        @on("vector")
                        def _(vector, qp=qp, ei=ei, ht=ht):
                            reR.wait_full(vector)
                            if ht == 0:
                                gbR.wait_full(vector)
                            vM.inc(vector.tensor_tensor(
                                sG[:, ht * 512:(ht + 1) * 512],
                                sRelu[reR.nc_ % 2][:], sGb[(gbR.np_ - 1) % 2][:],
                                op=ALU.mult), 1)
                            reR.consumed(vM)
                            if ht == NHT - 1:
                                gbR.consumed(vM)

                emit_w2()

                @on("tensor")
                def _(tensor, qp=qp, ei=ei, routed=routed):
                    if routed:
                        tensor.wait_ge(vM.h, vM.n)
                    else:
                        tensor.wait_ge(sM.h, sM.n)
                    w2R.wait_full(tensor)

                for tt in range(4):
                    for db in range(4):
                        @on("tensor")
                        def _(tensor, qp=qp, ei=ei, tt=tt, db=db, routed=routed):
                            pa2R.wait_free(tensor)
                            pa = pA20 if pa2R.np_ % 2 == 0 else pA21
                            started = False
                            if ei == 0:
                                tensor.matmul(pa[:], sOnesB[:],
                                              sSb2c[:, db * 512:(db + 1) * 512],
                                              start=True, stop=False)
                                started = True
                            ins = None
                            for hc in range(NHT):
                                ins = tensor.matmul(
                                    pa[:],
                                    sG[:, hc * 512 + tt * 128: hc * 512 + (tt + 1) * 128],
                                    sW2[:, hc * D + db * 512: hc * D + (db + 1) * 512],
                                    start=(not started and hc == 0),
                                    stop=(hc == NHT - 1 and ei != NEXP - 1))
                            if ei == NEXP - 1:
                                ins = tensor.matmul(
                                    pa[:],
                                    sGatesT[:, qp * 512 + tt * 128: qp * 512 + (tt + 1) * 128],
                                    sRb2B[:, db * 512:(db + 1) * 512],
                                    start=False, stop=True)
                            tM.inc(ins, 1)
                            pa2R.produced(tM)
                            if tt == 3 and db == 3:
                                w2R.consumed(tM)

                        @on("vector")
                        def _(vector, qp=qp, ei=ei, tt=tt, db=db):
                            pa2R.wait_full(vector)
                            pa = pA20 if pa2R.nc_ % 2 == 0 else pA21
                            dst = sAcc[:, tt * D + db * 512: tt * D + (db + 1) * 512]
                            if ei == 0:
                                vM.inc(vector.tensor_copy(dst, pa[:]), 1)
                            else:
                                vM.inc(vector.tensor_tensor(dst, pa[:], dst,
                                                            op=ALU.add), 1)
                            pa2R.consumed(vM)

            for tt in range(4):
                ti = qp * 4 + tt

                @on("sync")
                def _(sync, ti=ti):
                    x1tR.wait_free(sync)
                    dS.inc(sync.dma_start(out=sX1t[:], in_=x1s[ti]), 16)
                    x1tR.produced(dS)

                emit_ln_pre(lambda tt=tt: sAcc[:, tt * D:(tt + 1) * D])

                @on("vector")
                def _(vector, ti=ti, tt=tt):
                    vector.wait_ge(sM.h, sM.n)
                    x1tR.wait_full(vector)
                    x2R.wait_free(vector)
                    vM.inc(vector.tensor_scalar(sX2[:], sAcc[:, tt * D:(tt + 1) * D],
                                                sT1[:], sT0[:],
                                                op0=ALU.add, op1=ALU.mult), 1)
                    vector.wait_ge(vM.h, vM.n)
                    vM.inc(vector.tensor_tensor(sX2[:], sX2[:], sX1t[:],
                                                op=ALU.add), 1)
                    x1tR.consumed(vM)
                    x2R.produced(vM)

                @on("sync")
                def _(sync, ti=ti):
                    x2R.wait_full(sync)
                    dS.inc(sync.dma_start(out=y[ti * 128:(ti + 1) * 128, :],
                                          in_=sX2[:]), 16)
                    x2R.consumed(dS)

        @on("sync")
        def _(sync):
            sync.wait_ge(dS.h, dS.n)
            sync.wait_ge(dC.h, dC.n)

        ctxC.close()

    return nc


_NC_CACHE = {}


def _get_nc():
    key = DEBUG
    if key not in _NC_CACHE:
        _NC_CACHE[key] = build()
    return _NC_CACHE[key]


def _host_inputs(inputs):
    x = np.asarray(inputs["L0_attn_x"], dtype=np.float32)
    identN = np.eye(128).astype(ml_dtypes.bfloat16)
    base = dict(
        wq=np.asarray(inputs["L0_attn_Wq"], np.float32),
        wk=np.asarray(inputs["L0_attn_Wk"], np.float32),
        wv=np.asarray(inputs["L0_attn_Wv"], np.float32),
        wo=np.asarray(inputs["L0_attn_Wo"], np.float32),
        bq=np.asarray(inputs["L0_attn_bq"], np.float32).reshape(1, D),
        bk=np.asarray(inputs["L0_attn_bk"], np.float32).reshape(1, D),
        bv=np.asarray(inputs["L0_attn_bv"], np.float32).reshape(1, D),
        bo=np.asarray(inputs["L0_attn_bo"], np.float32).reshape(1, D),
        sw1=np.asarray(inputs["L0_ffn_sw1"], np.float32),
        sb1=np.asarray(inputs["L0_ffn_sb1"], np.float32),
        sw2=np.asarray(inputs["L0_ffn_sw2"], np.float32),
        sb2=np.asarray(inputs["L0_ffn_sb2"], np.float32),
        rw1=np.asarray(inputs["L0_ffn_rw1"], np.float32),
        rb1=np.asarray(inputs["L0_ffn_rb1"], np.float32),
        rw2=np.asarray(inputs["L0_ffn_rw2"], np.float32),
        rb2=np.asarray(inputs["L0_ffn_rb2"], np.float32),
        rww=np.asarray(inputs["L0_ffn_router_w"], np.float32),
        rwb=np.asarray(inputs["L0_ffn_router_b"], np.float32).reshape(1, 32),
        ident=identN,
    )
    in_maps = []
    mines = []
    for c in range(8):
        B, q = c // 4, c % 4
        xbat = np.ascontiguousarray(x[B])
        ks = np.arange(NBLK)
        t0s = QW * (q + 4 * ks)
        hcol = np.arange(H)[:, None, None] * 512
        slab = (t0s // 16)[None, :, None]
        rho = np.arange(16)[None, None, :]
        bandrows = (hcol + slab + rho).reshape(-1)
        minerows = (t0s[:, None] + np.arange(QW)[None, :]).reshape(-1)
        mines.append((B, minerows))
        p = np.arange(128)
        msk = np.zeros((NBLK, 128, 2048), np.float32)
        for k in range(NBLK):
            t0 = QW * (q + 4 * k)
            for cc in range(8):
                s0 = 128 * (8 * k + cc)
                jj = np.arange(QW)
                msk[k, :, cc * QW:(cc + 1) * QW] = (
                    (s0 + p[:, None]) <= (t0 + jj[None, :]))
        in_maps.append(dict(
            base,
            xb=xbat,
            xband=np.ascontiguousarray(xbat[bandrows]),
            xmine=np.ascontiguousarray(xbat[minerows]),
            masks=msk.astype(ml_dtypes.bfloat16),
        ))
    return x, in_maps, mines


def kernel(**inputs):
    x, in_maps, mines = _host_inputs(inputs)
    nc = _get_nc()
    res = run_bass_kernel_spmd(nc, in_maps, core_ids=list(range(8)))
    out = np.empty_like(x)
    for c in range(8):
        B, minerows = mines[c]
        out[B, minerows, :] = np.asarray(res.results[c]["y"], np.float32)
    return out


if __name__ == "__main__":
    import reference as R
    ins = {k: np.asarray(v) for k, v in R.setup_inputs(0).items()}
    o = kernel(**ins)
    print("out", o.shape, o.dtype)


# version 06
# speedup vs baseline: 1.2756x; 1.2756x over previous
import numpy as np
import ml_dtypes
import concourse.bass as bass
import concourse.mybir as mybir
from concourse.bass_utils import run_bass_kernel_spmd
from contextlib import ExitStack

F32 = mybir.dt.float32
BF16 = mybir.dt.bfloat16
AF = mybir.ActivationFunctionType
AX = mybir.AxisListType
ALU = mybir.AluOpType

S, D = 8192, 2048
H, DH = 16, 128
DF = 1408
NHT = 11
NCH = 16
NBLK = 8
QW = 256
SCALE = float(1.0 / np.sqrt(128.0))
EPS = 1e-5

DEBUG = 0


class Sem:
    def __init__(self, ctx, nc, name):
        self.h = ctx.enter_context(nc.semaphore(name))
        self.n = 0

    def inc(self, instr, k=1):
        instr.then_inc(self.h, k)
        self.n += k
        return instr


class Ring:

    def __init__(self, depth):
        self.depth = depth
        self.pv = []
        self.cv = []

    def wait_free(self, eng, ahead=0):
        i = len(self.pv) + ahead
        if i >= self.depth:
            s, v = self.cv[i - self.depth]
            eng.wait_ge(s.h, v)

    def produced(self, sem):
        self.pv.append((sem, sem.n))

    def wait_full(self, eng):
        s, v = self.pv[len(self.cv)]
        eng.wait_ge(s.h, v)

    def consumed(self, sem):
        self.cv.append((sem, sem.n))

    @property
    def np_(self):
        return len(self.pv)

    @property
    def nc_(self):
        return len(self.cv)


def build():
    nc = bass.Bass("TRN2", target_bir_lowering=False)

    xb = nc.dram_tensor("xb", [S, D], F32, kind="ExternalInput").ap()
    xband = nc.dram_tensor("xband", [2048, D], F32, kind="ExternalInput").ap()
    xmine = nc.dram_tensor("xmine", [2048, D], F32, kind="ExternalInput").ap()
    masksI = nc.dram_tensor("masks", [NBLK, 128, 2048], BF16, kind="ExternalInput").ap()
    ident = nc.dram_tensor("ident", [128, 128], BF16, kind="ExternalInput").ap()
    wq = nc.dram_tensor("wq", [D, D], F32, kind="ExternalInput").ap()
    wk = nc.dram_tensor("wk", [D, D], F32, kind="ExternalInput").ap()
    wv = nc.dram_tensor("wv", [D, D], F32, kind="ExternalInput").ap()
    wo = nc.dram_tensor("wo", [D, D], F32, kind="ExternalInput").ap()
    bq = nc.dram_tensor("bq", [1, D], F32, kind="ExternalInput").ap()
    bk = nc.dram_tensor("bk", [1, D], F32, kind="ExternalInput").ap()
    bv = nc.dram_tensor("bv", [1, D], F32, kind="ExternalInput").ap()
    bo = nc.dram_tensor("bo", [1, D], F32, kind="ExternalInput").ap()
    sw1 = nc.dram_tensor("sw1", [2, D, DF], F32, kind="ExternalInput").ap()
    sb1 = nc.dram_tensor("sb1", [2, DF], F32, kind="ExternalInput").ap()
    sw2 = nc.dram_tensor("sw2", [2, DF, D], F32, kind="ExternalInput").ap()
    sb2 = nc.dram_tensor("sb2", [2, D], F32, kind="ExternalInput").ap()
    rw1 = nc.dram_tensor("rw1", [32, D, DF], F32, kind="ExternalInput").ap()
    rb1 = nc.dram_tensor("rb1", [32, DF], F32, kind="ExternalInput").ap()
    rw2 = nc.dram_tensor("rw2", [32, DF, D], F32, kind="ExternalInput").ap()
    rb2 = nc.dram_tensor("rb2", [32, D], F32, kind="ExternalInput").ap()
    rww = nc.dram_tensor("rww", [D, 32], F32, kind="ExternalInput").ap()
    rwb = nc.dram_tensor("rwb", [1, 32], F32, kind="ExternalInput").ap()

    y = nc.dram_tensor("y", [2048, D], F32, kind="ExternalOutput").ap()

    kscr = nc.dram_tensor("kscr", [H, 128, S], BF16, kind="Internal").ap()
    vscr = nc.dram_tensor("vscr", [H, 128, S], BF16, kind="Internal").ap()
    qscr = nc.dram_tensor("qscr", [H, 128, 2048], BF16, kind="Internal").ap()
    ascr = nc.dram_tensor("ascr", [H, 128, 2048], BF16, kind="Internal").ap()
    h2ts = nc.dram_tensor("h2ts", [NCH, 128, 2048], BF16, kind="Internal").ap()
    x1s = nc.dram_tensor("x1s", [16, 128, D], F32, kind="Internal").ap()
    wscr1 = nc.dram_tensor("wscr1", [34, 6, 128, 4096], BF16, kind="Internal").ap()
    wscr2 = nc.dram_tensor("wscr2", [34, 128, NHT * D], BF16, kind="Internal").ap()

    if DEBUG >= 1:
        dbg_k = nc.dram_tensor("dbg_k", [H, 128, S], BF16, kind="ExternalOutput").ap()
        dbg_v = nc.dram_tensor("dbg_v", [H, 128, S], BF16, kind="ExternalOutput").ap()
        dbg_q = nc.dram_tensor("dbg_q", [H, 128, 2048], BF16, kind="ExternalOutput").ap()
    if DEBUG >= 2:
        dbg_at = nc.dram_tensor("dbg_at", [H, 128, 2048], BF16, kind="ExternalOutput").ap()
    if DEBUG >= 3:
        dbg_x1 = nc.dram_tensor("dbg_x1", [16, 128, D], F32, kind="ExternalOutput").ap()
        dbg_h2t = nc.dram_tensor("dbg_h2t", [NCH, 128, 2048], BF16, kind="ExternalOutput").ap()
        dbg_g = nc.dram_tensor("dbg_g", [32, 2048], BF16, kind="ExternalOutput").ap()

    ctx = ExitStack()
    e = ctx.enter_context
    with ctx:
        sOnesB = e(nc.sbuf_tensor("sOnesB", [1, 128], BF16))
        sOnesF = e(nc.sbuf_tensor("sOnesF", [1, 128], F32))
        sOnesC = e(nc.sbuf_tensor("sOnesC", [128, 1], BF16))
        sIdent = e(nc.sbuf_tensor("sIdent", [128, 128], BF16))
        sIdentF = e(nc.sbuf_tensor("sIdentF", [128, 128], F32))
        sBqT = e(nc.sbuf_tensor("sBqT", [128, 16], F32))
        sBkT = e(nc.sbuf_tensor("sBkT", [128, 16], F32))
        sBvR = e(nc.sbuf_tensor("sBvR", [1, D], BF16))
        sBoR = e(nc.sbuf_tensor("sBoR", [1, D], BF16))
        sRb1T = e(nc.sbuf_tensor("sRb1T", [128, NHT * 32], F32))
        sHsb1T = e(nc.sbuf_tensor("sHsb1T", [128, NHT * 2], F32))
        sSb2c = e(nc.sbuf_tensor("sSb2c", [1, D], BF16))
        sRb2B = e(nc.sbuf_tensor("sRb2B", [32, D], BF16))
        sRwB = e(nc.sbuf_tensor("sRwB", [128, NCH * 32], BF16))
        sRwbT = e(nc.sbuf_tensor("sRwbT", [32, 1], F32))
        sGatesT = e(nc.sbuf_tensor("sGatesT", [32, 2048], BF16))
        sGRow = e(nc.sbuf_tensor("sGRow", [1, 512], BF16))
        sMV = e(nc.sbuf_tensor("sMV", [128, 2], F32))
        sStat = e(nc.sbuf_tensor("sStat", [128, 24], F32))
        sT0 = e(nc.sbuf_tensor("sT0", [128, 1], F32))
        sT1 = e(nc.sbuf_tensor("sT1", [128, 1], F32))
        sT2 = e(nc.sbuf_tensor("sT2", [128, 1], F32))
        sMx = e(nc.sbuf_tensor("sMx", [1, 24], F32))
        sSc0 = e(nc.sbuf_tensor("sSc0", [1, 1], F32))
        sSc1 = e(nc.sbuf_tensor("sSc1", [1, 1], F32))
        sBias = e(nc.sbuf_tensor("sBias", [128, 1], F32))

        dS = Sem(ctx, nc, "dS")
        dC = Sem(ctx, nc, "dC")
        tM = Sem(ctx, nc, "tM")
        vM = Sem(ctx, nc, "vM")
        sM = Sem(ctx, nc, "sM")
        pM = Sem(ctx, nc, "pM")
        ALL = (tM, vM, sM, pM, dS, dC)

        block = e(nc.Block())

        def on(name):
            dec = getattr(block, name)

            def run(fn):
                dec(fn)
            return run

        def barrier():
            counts = [(s, s.n) for s in ALL]
            for eng in ("tensor", "vector", "scalar", "gpsimd", "sync"):
                @on(eng)
                def _(E, counts=counts):
                    for s, v in counts:
                        if v:
                            E.wait_ge(s.h, v)

        def emit_ln_pre(src_fn):
            @on("vector")
            def _(vector):
                vector.wait_ge(sM.h, sM.n)
                vector.wait_ge(vM.h, vM.n)
                src = src_fn()
                for j in range(3):
                    vector.bn_stats(sStat[:, 6 * j:6 * j + 6],
                                    src[:, 512 * j:512 * (j + 1)])
                vM.inc(vector.bn_stats(sStat[:, 18:24], src[:, 1536:2048]), 1)
                vector.wait_ge(vM.h, vM.n)
                vM.inc(vector.bn_aggr(sMV[:], sStat[:]), 1)
                vector.wait_ge(vM.h, vM.n)
                vM.inc(vector.tensor_scalar(sT2[:], sMV[:, 1:2], EPS, None,
                                            op0=ALU.add), 1)
                vector.wait_ge(vM.h, vM.n)
                vM.inc(vector.reciprocal(sT0[:], sT2[:]), 1)
                vector.wait_ge(vM.h, vM.n)
                vM.inc(vector.tensor_scalar(sT1[:], sMV[:, 0:1], -1.0, None,
                                            op0=ALU.mult), 1)

            @on("scalar")
            def _(scalar):
                scalar.wait_ge(vM.h, vM.n)
                sM.inc(scalar.activation(sT0[:], sT0[:], AF.Sqrt), 1)

        def emit_wcast(wsrc, wdst, stg):
            stgRw = Ring(2)
            for ch in range(NCH):
                @on("sync")
                def _(sync, ch=ch):
                    stgRw.wait_free(sync)
                    dS.inc(sync.dma_start(out=stg[stgRw.np_ % 2][:],
                                          in_=wsrc[ch * 128:(ch + 1) * 128, :]), 16)
                    stgRw.produced(dS)

                @on("gpsimd")
                def _(g, ch=ch):
                    stgRw.wait_full(g)
                    pM.inc(g.tensor_copy(wdst[:, ch * D:(ch + 1) * D],
                                         stg[stgRw.nc_ % 2][:]), 1)
                    stgRw.consumed(pM)

        ctx0 = ExitStack()
        e0 = ctx0.enter_context
        sBrQ = e0(nc.sbuf_tensor("sBrQ", [1, D], F32))
        sBrK = e0(nc.sbuf_tensor("sBrK", [1, D], F32))
        sBrV = e0(nc.sbuf_tensor("sBrV", [1, D], F32))
        sBrO = e0(nc.sbuf_tensor("sBrO", [1, D], F32))
        sSb1r0 = e0(nc.sbuf_tensor("sSb1r0", [1, DF], F32))
        sSb1r1 = e0(nc.sbuf_tensor("sSb1r1", [1, DF], F32))
        sSb2r0 = e0(nc.sbuf_tensor("sSb2r0", [1, D], F32))
        sSb2r1 = e0(nc.sbuf_tensor("sSb2r1", [1, D], F32))
        sRb1F = e0(nc.sbuf_tensor("sRb1F", [32, DF], F32))
        sRb2F = e0(nc.sbuf_tensor("sRb2F", [32, D], F32))
        sRwF = e0(nc.sbuf_tensor("sRwF", [128, NCH * 32], F32))
        ctx0p = ExitStack()
        pF0 = ctx0p.enter_context(nc.psum_tensor("pF0", [128, 512], F32))
        pQ0 = ctx0p.enter_context(nc.psum_tensor("pQ0", [128, 512], F32))
        pQ1 = ctx0p.enter_context(nc.psum_tensor("pQ1", [128, 512], F32))
        pp0R = Ring(2)

        @on("sync")
        def _(sync):
            dS.inc(sync.dma_start(out=sIdent[:], in_=ident), 16)

        @on("scalar")
        def _(scalar):
            for dst, src in ((sBrQ, bq), (sBrK, bk), (sBrV, bv), (sBrO, bo),
                             (sSb1r0, sb1[0:1, :]), (sSb1r1, sb1[1:2, :]),
                             (sSb2r0, sb2[0:1, :]), (sSb2r1, sb2[1:2, :]),
                             (sRb1F, rb1), (sRb2F, rb2)):
                dC.inc(scalar.dma_start(out=dst[:], in_=src), 16)
            dC.inc(scalar.dma_start(
                out=sRwF[:], in_=rww.rearrange("(c p) e -> p c e", p=128)), 16)
            dC.inc(scalar.dma_start(
                out=sRwbT[:], in_=rwb.rearrange("o e -> e o")), 16)

        @on("vector")
        def _(vector):
            vector.memset(sOnesB[:], 1.0)
            vector.memset(sOnesF[:], 1.0)
            vector.wait_ge(dS.h, 16)
            vector.tensor_copy(sIdentF[:], sIdent[:])
            vM.inc(vector.memset(sOnesC[:], 1.0), 1)
            vector.wait_ge(dC.h, 16 * 12)
            vector.tensor_copy(sBvR[:], sBrV[:])
            vector.tensor_copy(sBoR[:], sBrO[:])
            vM.inc(vector.tensor_tensor(sSb2r0[:], sSb2r0[:], sSb2r1[:],
                                        op=ALU.add), 1)
            vector.wait_ge(vM.h, vM.n)
            vector.tensor_scalar(sSb2c[:], sSb2r0[:], 0.5, None, op0=ALU.mult)
            vector.tensor_copy(sRb2B[:], sRb2F[:])
            vM.inc(vector.tensor_copy(sRwB[:], sRwF[:]), 1)

        jobs = [(sBrQ, 16, 0), (sBrK, 16, 16), (sSb1r0, NHT, 32), (sSb1r1, NHT, 48)]

        @on("tensor")
        def _(tensor):
            tensor.wait_ge(dC.h, 16 * 12)
            tensor.wait_ge(vM.h, 1)
            first = True
            for srcT, n, base in jobs:
                for i in range(n):
                    tensor.matmul(pF0[:, base + i:base + i + 1],
                                  srcT[:, i * 128:(i + 1) * 128],
                                  sOnesF[:, 0:1], start=first, stop=False)
                    first = False
            tM.inc(tensor.matmul(pF0[0:1, 60:61], sOnesF[0:1, 0:1], sOnesF[0:1, 0:1],
                                 start=False, stop=True), 1)

        @on("vector")
        def _(vector):
            vector.wait_ge(tM.h, tM.n)
            for srcT, n, base in jobs:
                for i in range(n):
                    if base == 0:
                        vector.tensor_copy(sBqT[:, i:i + 1], pF0[:, i:i + 1])
                    elif base == 16:
                        vector.tensor_copy(sBkT[:, i:i + 1], pF0[:, 16 + i:17 + i])
                    elif base == 32:
                        vector.tensor_copy(sHsb1T[:, 2 * i:2 * i + 1],
                                           pF0[:, 32 + i:33 + i])
                    else:
                        vM.inc(vector.tensor_copy(sHsb1T[:, 2 * i + 1:2 * i + 2],
                                                  pF0[:, 48 + i:49 + i]), 1)
            vector.wait_ge(vM.h, vM.n)
            vM.inc(vector.tensor_scalar(sHsb1T[:], sHsb1T[:], 0.5, None,
                                        op0=ALU.mult), 1)

        for ht in range(NHT):
            @on("tensor")
            def _(tensor, ht=ht):
                pp0R.wait_free(tensor)
                tM.inc(tensor.transpose(
                    (pQ0 if pp0R.np_ % 2 == 0 else pQ1)[:, 0:32],
                    sRb1F[:, ht * 128:(ht + 1) * 128], sIdentF[0:32, 0:32]), 1)
                pp0R.produced(tM)

            @on("vector")
            def _(vector, ht=ht):
                pp0R.wait_full(vector)
                vM.inc(vector.tensor_copy(
                    sRb1T[:, ht * 32:(ht + 1) * 32],
                    (pQ0 if pp0R.nc_ % 2 == 0 else pQ1)[:, 0:32]), 1)
                pp0R.consumed(vM)

        barrier()
        ctx0p.close()
        ctx0.close()

        def emit_xpass(ntiles, src_of, sXl, sH1l, dst4_of, tile_done, psA, psB,
                       pre_evac_wait=None):
            xR = Ring(2)
            h1R = Ring(2)
            tpRl = Ring(2)
            for t in range(ntiles):
                @on("sync")
                def _(sync, t=t):
                    xR.wait_free(sync)
                    dS.inc(sync.dma_start(out=sXl[t % 2][:], in_=src_of(t)), 16)
                    xR.produced(dS)

                @on("vector")
                def _(vector, t=t):
                    xR.wait_full(vector)

                emit_ln_pre(lambda t=t: sXl[t % 2][:])

                @on("vector")
                def _(vector, t=t):
                    vector.wait_ge(sM.h, sM.n)
                    h1R.wait_free(vector)
                    vM.inc(vector.tensor_scalar(sH1l[t % 2][:], sXl[t % 2][:],
                                                sT1[:], sT0[:],
                                                op0=ALU.add, op1=ALU.mult), 1)
                    h1R.produced(vM)
                    xR.consumed(vM)

                for c4 in range(4):
                    @on("tensor")
                    def _(tensor, t=t, c4=c4):
                        if c4 == 0:
                            h1R.wait_full(tensor)
                        tpRl.wait_free(tensor)
                        pb = psA if tpRl.np_ % 2 == 0 else psB
                        ins = None
                        for j in range(4):
                            ins = tensor.transpose(
                                pb[:, j * 128:(j + 1) * 128],
                                sH1l[t % 2][:, (c4 * 4 + j) * 128:(c4 * 4 + j + 1) * 128],
                                sIdent[:])
                        tM.inc(ins, 1)
                        tpRl.produced(tM)

                    @on("vector")
                    def _(vector, t=t, c4=c4):
                        if c4 == 0 and pre_evac_wait is not None:
                            pre_evac_wait(vector, t)
                        tpRl.wait_full(vector)
                        pb = psA if tpRl.nc_ % 2 == 0 else psB
                        vM.inc(vector.tensor_copy(dst4_of(t, c4), pb[:]), 1)
                        tpRl.consumed(vM)
                        if c4 == 3:
                            h1R.consumed(vM)
                            tile_done(t)

        ctxQ = ExitStack()
        eQ = ctxQ.enter_context
        sWq = eQ(nc.sbuf_tensor("sWq", [128, NCH * D], BF16))
        stgQ = [eQ(nc.sbuf_tensor(f"stgQ{i}", [128, D], F32)) for i in range(2)]
        sXq = [eQ(nc.sbuf_tensor(f"sXq{i}", [128, D], F32)) for i in range(2)]
        sH1q = [eQ(nc.sbuf_tensor(f"sH1q{i}", [128, D], BF16)) for i in range(2)]
        sHbT = eQ(nc.sbuf_tensor("sHbT", [128, NCH * 128], BF16))
        sQbT = [eQ(nc.sbuf_tensor(f"sQbT{i}", [128, 2048], BF16)) for i in range(2)]
        ctxQp = ExitStack()
        pPl = [ctxQp.enter_context(nc.psum_tensor(f"pP{i}", [128, 512], F32))
               for i in range(4)]
        pTa = ctxQp.enter_context(nc.psum_tensor("pTa", [128, 512], BF16))
        pTb = ctxQp.enter_context(nc.psum_tensor("pTb", [128, 512], BF16))

        emit_wcast(wq, sWq, stgQ)

        @on("tensor")
        def _(tensor):
            tensor.wait_ge(pM.h, pM.n)

        ppR = Ring(4)
        hbtR = Ring(1)
        qbR = Ring(2)

        for hb in range(16):
            def dst4_of(t, c4):
                return sHbT[:, c4 * 512:(c4 + 1) * 512]

            def tile_done(t):
                hbtR.produced(vM)

            def pre_wait(vector, t):
                hbtR.wait_free(vector)

            emit_xpass(1, lambda t, hb=hb: xband[hb * 128:(hb + 1) * 128, :],
                       sXq, sH1q, dst4_of, tile_done, pTa, pTb, pre_wait)

            for c in range(NCH):
                @on("tensor")
                def _(tensor, hb=hb, c=c):
                    if c == 0:
                        hbtR.wait_full(tensor)
                        qbR.wait_free(tensor)
                    ppR.wait_free(tensor)
                    pp = pPl[ppR.np_ % 4]
                    ins = None
                    for ch in range(NCH):
                        ins = tensor.matmul(
                            pp[:, 0:128],
                            sWq[:, ch * D + c * 128: ch * D + (c + 1) * 128],
                            sHbT[:, ch * 128:(ch + 1) * 128],
                            start=(ch == 0), stop=(ch == NCH - 1))
                    tM.inc(ins, 1)
                    ppR.produced(tM)
                    if c == NCH - 1:
                        hbtR.consumed(tM)

                @on("vector")
                def _(vector, hb=hb, c=c):
                    ppR.wait_full(vector)
                    pp = pPl[ppR.nc_ % 4]
                    dst = sQbT[hb % 2][:].rearrange(
                        "p (k r c2) -> p k r c2", k=8, r=16)[:, :, :, c:c + 1]
                    src = pp[:, 0:128].rearrange("p (k r c2) -> p k r c2",
                                                 k=8, r=16, c2=1)
                    vM.inc(vector.tensor_scalar(dst, src, sBqT[:, c:c + 1],
                                                None, op0=ALU.add), 1)
                    ppR.consumed(vM)
                    if c == NCH - 1:
                        qbR.produced(vM)

            @on("sync")
            def _(sync, hb=hb):
                qbR.wait_full(sync)
                dS.inc(sync.dma_start(out=qscr[hb], in_=sQbT[hb % 2][:]), 16)
                qbR.consumed(dS)

        barrier()
        ctxQp.close()
        ctxQ.close()

        ctxK = ExitStack()
        eK = ctxK.enter_context
        sWk = eK(nc.sbuf_tensor("sWk", [128, NCH * D], BF16))
        stgK = [eK(nc.sbuf_tensor(f"stgK{i}", [128, D], F32)) for i in range(2)]
        sXk = [eK(nc.sbuf_tensor(f"sXk{i}", [128, D], F32)) for i in range(2)]
        sH1k = [eK(nc.sbuf_tensor(f"sH1k{i}", [128, D], BF16)) for i in range(2)]
        sH1T = [eK(nc.sbuf_tensor(f"sH1T{i}", [128, NCH * 512], BF16)) for i in range(2)]
        sKst = eK(nc.sbuf_tensor("sKst", [128, H * 512], BF16))
        ctxKp = ExitStack()
        pPl = [ctxKp.enter_context(nc.psum_tensor(f"pP{i}k", [128, 512], F32))
               for i in range(4)]
        pTa = ctxKp.enter_context(nc.psum_tensor("pTak", [128, 512], BF16))
        pTb = ctxKp.enter_context(nc.psum_tensor("pTbk", [128, 512], BF16))

        emit_wcast(wk, sWk, stgK)

        @on("tensor")
        def _(tensor):
            tensor.wait_ge(pM.h, pM.n)

        ppR = Ring(4)
        h1aR = Ring(2)
        kstR = Ring(1)

        for sb in range(16):
            def dst4_of(t, c4, sb=sb):
                return sH1T[sb % 2][:].rearrange("p (c f) -> p c f", c=NCH)[
                    :, c4 * 4:(c4 + 1) * 4, t * 128:(t + 1) * 128]

            def tile_done(t):
                if t == 3:
                    h1aR.produced(vM)

            def pre_wait(vector, t):
                if t == 0:
                    h1aR.wait_free(vector)

            emit_xpass(4, lambda t, sb=sb: xb[(sb * 4 + t) * 128:(sb * 4 + t + 1) * 128, :],
                       sXk, sH1k, dst4_of, tile_done, pTa, pTb, pre_wait)

            for h in range(H):
                @on("tensor")
                def _(tensor, sb=sb, h=h):
                    if h == 0:
                        h1aR.wait_full(tensor)
                        kstR.wait_free(tensor)
                    ppR.wait_free(tensor)
                    pp = pPl[ppR.np_ % 4]
                    ins = None
                    for ch in range(NCH):
                        ins = tensor.matmul(
                            pp[:, 0:512],
                            sWk[:, ch * D + h * 128: ch * D + (h + 1) * 128],
                            sH1T[sb % 2][:, ch * 512:(ch + 1) * 512],
                            start=(ch == 0), stop=(ch == NCH - 1))
                    tM.inc(ins, 1)
                    ppR.produced(tM)
                    if h == H - 1:
                        h1aR.consumed(tM)

                @on("vector")
                def _(vector, sb=sb, h=h):
                    ppR.wait_full(vector)
                    pp = pPl[ppR.nc_ % 4]
                    vM.inc(vector.tensor_scalar(sKst[:, h * 512:(h + 1) * 512],
                                                pp[:, 0:512], sBkT[:, h:h + 1],
                                                None, op0=ALU.add), 1)
                    ppR.consumed(vM)
                    if h == H - 1:
                        kstR.produced(vM)

            @on("sync")
            def _(sync, sb=sb):
                kstR.wait_full(sync)
                dS.inc(sync.dma_start(
                    out=kscr[:, :, sb * 512:(sb + 1) * 512].rearrange("h p s -> p h s"),
                    in_=sKst[:].rearrange("p (h s) -> p h s", s=512)), 16)
                kstR.consumed(dS)

        barrier()
        ctxKp.close()
        ctxK.close()

        ctxV = ExitStack()
        eV = ctxV.enter_context
        sWv = eV(nc.sbuf_tensor("sWv", [128, NCH * D], BF16))
        stgV = [eV(nc.sbuf_tensor(f"stgV{i}", [128, D], F32)) for i in range(2)]
        sXv = [eV(nc.sbuf_tensor(f"sXv{i}", [128, D], F32)) for i in range(2)]
        sH1v = [eV(nc.sbuf_tensor(f"sH1v{i}", [128, D], BF16)) for i in range(2)]
        sH1Tv = [eV(nc.sbuf_tensor(f"sH1Tv{i}", [128, NCH * 512], BF16)) for i in range(2)]
        sVst = [eV(nc.sbuf_tensor(f"sVst{i}", [128, D], BF16)) for i in range(2)]
        ctxVp = ExitStack()
        pPl = [ctxVp.enter_context(nc.psum_tensor(f"pP{i}v", [128, 512], F32))
               for i in range(4)]
        pTa = ctxVp.enter_context(nc.psum_tensor("pTav", [128, 512], BF16))
        pTb = ctxVp.enter_context(nc.psum_tensor("pTbv", [128, 512], BF16))

        emit_wcast(wv, sWv, stgV)

        @on("tensor")
        def _(tensor):
            tensor.wait_ge(pM.h, pM.n)

        ppR = Ring(4)
        h1aR = Ring(2)
        vstR = Ring(2)

        for sb in range(16):
            def dst4_of(t, c4, sb=sb):
                return sH1Tv[sb % 2][:].rearrange("p (c f) -> p c f", c=NCH)[
                    :, c4 * 4:(c4 + 1) * 4, t * 128:(t + 1) * 128]

            def tile_done(t):
                if t == 3:
                    h1aR.produced(vM)

            def pre_wait(vector, t):
                if t == 0:
                    h1aR.wait_free(vector)

            emit_xpass(4, lambda t, sb=sb: xb[(sb * 4 + t) * 128:(sb * 4 + t + 1) * 128, :],
                       sXv, sH1v, dst4_of, tile_done, pTa, pTb, pre_wait)

            for tt in range(4):
                for nb in range(4):
                    @on("tensor")
                    def _(tensor, sb=sb, tt=tt, nb=nb):
                        if tt == 0 and nb == 0:
                            h1aR.wait_full(tensor)
                        if nb == 0:
                            vstR.wait_free(tensor)
                        ppR.wait_free(tensor)
                        pp = pPl[ppR.np_ % 4]
                        tensor.matmul(pp[:, 0:512], sOnesB[:],
                                      sBvR[:, nb * 512:(nb + 1) * 512],
                                      start=True, stop=False)
                        ins = None
                        for ch in range(NCH):
                            ins = tensor.matmul(
                                pp[:, 0:512],
                                sH1Tv[sb % 2][:, ch * 512 + tt * 128: ch * 512 + (tt + 1) * 128],
                                sWv[:, ch * D + nb * 512: ch * D + (nb + 1) * 512],
                                start=False, stop=(ch == NCH - 1))
                        tM.inc(ins, 1)
                        ppR.produced(tM)
                        if tt == 3 and nb == 3:
                            h1aR.consumed(tM)

                    @on("vector")
                    def _(vector, sb=sb, tt=tt, nb=nb):
                        ppR.wait_full(vector)
                        pp = pPl[ppR.nc_ % 4]
                        vM.inc(vector.tensor_copy(
                            sVst[tt % 2][:, nb * 512:(nb + 1) * 512],
                            pp[:, 0:512]), 1)
                        ppR.consumed(vM)
                        if nb == 3:
                            vstR.produced(vM)

                @on("sync")
                def _(sync, sb=sb, tt=tt):
                    vstR.wait_full(sync)
                    st = sb * 4 + tt
                    dS.inc(sync.dma_start(
                        out=vscr[:, :, st * 128:(st + 1) * 128].rearrange("h p d -> p h d"),
                        in_=sVst[tt % 2][:].rearrange("p (h d) -> p h d", d=128)), 16)
                    vstR.consumed(dS)

        if DEBUG >= 1:
            barrier()

            @on("sync")
            def _(sync):
                for hh in range(H):
                    dS.inc(sync.dma_start(out=dbg_k[hh], in_=kscr[hh]), 16)
                    dS.inc(sync.dma_start(out=dbg_v[hh], in_=vscr[hh]), 16)
                    dS.inc(sync.dma_start(out=dbg_q[hh], in_=qscr[hh]), 16)
                sync.wait_ge(dS.h, dS.n)

        barrier()
        ctxVp.close()
        ctxV.close()

        ctxB = ExitStack()
        eB = ctxB.enter_context
        sK = [eB(nc.sbuf_tensor(f"sK{i}", [128, S], BF16)) for i in range(2)]
        sV = [eB(nc.sbuf_tensor(f"sV{i}", [128, S], BF16)) for i in range(2)]
        sQ = [eB(nc.sbuf_tensor(f"sQ{i}", [128, 2048], BF16)) for i in range(2)]
        sMk = [eB(nc.sbuf_tensor(f"sMk{i}", [128, 2048], BF16)) for i in range(2)]
        sPt = [eB(nc.sbuf_tensor(f"sPt{i}", [128, 1024], BF16)) for i in range(3)]
        sQsq = eB(nc.sbuf_tensor("sQsq", [128, 2048], BF16))
        sRL = eB(nc.sbuf_tensor("sRL", [1, 256], F32))
        sBcB = eB(nc.sbuf_tensor("sBcB", [128, 256], BF16))
        sAo = [eB(nc.sbuf_tensor(f"sAo{i}", [128, 256], BF16)) for i in range(2)]
        ctxB1 = ExitStack()
        pS = [ctxB1.enter_context(nc.psum_tensor(f"pS{i}", [128, 1024], F32))
              for i in range(3)]
        pOut = ctxB1.enter_context(nc.psum_tensor("pOut", [128, 256], F32))
        pMisc = ctxB1.enter_context(nc.psum_tensor("pMisc", [128, 512], F32))
        pL = pMisc[:1, 256:512]
        pBc = pMisc[:, 0:256]
        pRow = pMisc[0:1, 0:512]

        kR = Ring(2)
        vR = Ring(2)
        qR = Ring(2)
        mkR = Ring(2)
        ptR = Ring(3)
        psR = Ring(3)
        aoR = Ring(2)

        for h in range(H):
            @on("sync")
            def _(sync, h=h):
                for ring, dst, src in ((kR, sK, kscr), (vR, sV, vscr), (qR, sQ, qscr)):
                    ring.wait_free(sync)
                    dS.inc(sync.dma_start(out=dst[h % 2][:], in_=src[h]), 16)
                    ring.produced(dS)

            @on("scalar")
            def _(scalar, h=h):
                qR.wait_full(scalar)
                scalar.wait_ge(tM.h, tM.n)
                sM.inc(scalar.activation(sQsq[:], sQ[h % 2][:], AF.Square), 1)

            @on("tensor")
            def _(tensor, h=h):
                tensor.wait_ge(sM.h, sM.n)
                for j in range(4):
                    tensor.wait_ge(vM.h, vM.n + j)
                    tM.inc(tensor.matmul(pRow[:, 0:512], sOnesC[:],
                                         sQsq[:, j * 512:(j + 1) * 512],
                                         start=True, stop=True), 1)

            @on("vector")
            def _(vector, h=h):
                for j in range(4):
                    vector.wait_ge(tM.h, tM.n - 3 + j)
                    vM.inc(vector.reduce_max(sMx[:, j:j + 1], pRow[:, 0:512],
                                             axis=AX.X), 1)

            for j4 in range(4):
                @on("scalar")
                def _(scalar, h=h, j4=j4):
                    if j4 == 0:
                        kR.wait_full(scalar)
                    scalar.wait_ge(tM.h, tM.n)
                    sM.inc(scalar.activation(
                        sQsq[:], sK[h % 2][:, j4 * 2048:(j4 + 1) * 2048],
                        AF.Square), 1)

                @on("tensor")
                def _(tensor, h=h, j4=j4):
                    tensor.wait_ge(sM.h, sM.n)
                    for j in range(4):
                        tensor.wait_ge(vM.h, vM.n + j)
                        tM.inc(tensor.matmul(pRow[:, 0:512], sOnesC[:],
                                             sQsq[:, j * 512:(j + 1) * 512],
                                             start=True, stop=True), 1)

                @on("vector")
                def _(vector, h=h, j4=j4):
                    for j in range(4):
                        vector.wait_ge(tM.h, tM.n - 3 + j)
                        vM.inc(vector.reduce_max(
                            sMx[:, 4 + j4 * 4 + j:5 + j4 * 4 + j],
                            pRow[:, 0:512], axis=AX.X), 1)

            @on("vector")
            def _(vector, h=h):
                vector.wait_ge(vM.h, vM.n)
                vector.reduce_max(sSc0[:], sMx[:, 0:4], axis=AX.X)
                vM.inc(vector.reduce_max(sSc1[:], sMx[:, 4:20], axis=AX.X), 1)
                vector.wait_ge(vM.h, vM.n)
                vM.inc(vector.tensor_tensor(sSc0[:], sSc0[:], sSc1[:],
                                            op=ALU.mult), 1)

            @on("scalar")
            def _(scalar, h=h):
                scalar.wait_ge(vM.h, vM.n)
                sM.inc(scalar.activation(sSc0[:], sSc0[:], AF.Sqrt), 1)

            @on("vector")
            def _(vector, h=h):
                vector.wait_ge(sM.h, sM.n)
                vM.inc(vector.tensor_scalar(sSc0[:], sSc0[:], -SCALE, None,
                                            op0=ALU.mult), 1)

            @on("tensor")
            def _(tensor, h=h):
                tensor.wait_ge(vM.h, vM.n)
                tM.inc(tensor.matmul(pBc[:, 0:1], sOnesF[:], sSc0[:],
                                     start=True, stop=True), 1)

            @on("vector")
            def _(vector, h=h):
                vector.wait_ge(tM.h, tM.n)
                vM.inc(vector.tensor_copy(sBias[:], pBc[:, 0:1]), 1)
            sbias_ev = vM.n

            for k in range(NBLK):
                ngr = 2 * k + 2

                @on("sync")
                def _(sync, h=h, k=k):
                    mkR.wait_free(sync)
                    dS.inc(sync.dma_start(out=sMk[mkR.np_ % 2][:],
                                          in_=masksI[k]), 16)
                    mkR.produced(dS)

                for s0 in range(0, ngr, 3):
                    batch = list(range(s0, min(s0 + 3, ngr)))

                    @on("tensor")
                    def _(tensor, h=h, k=k, batch=batch):
                        for g in batch:
                            if g == 0:
                                vR.wait_full(tensor)
                            psR.wait_free(tensor)
                            ps = pS[psR.np_ % 3]
                            ins = None
                            for i in range(4):
                                c = 4 * g + i
                                ins = tensor.matmul(
                                    ps[:, i * 256:(i + 1) * 256],
                                    sK[h % 2][:, c * 128:(c + 1) * 128],
                                    sQ[h % 2][:, k * 256:(k + 1) * 256],
                                    start=True, stop=True)
                            tM.inc(ins, 1)
                            psR.produced(tM)

                    @on("scalar")
                    def _(scalar, h=h, k=k, batch=batch, ngr=ngr,
                          sbias_ev=sbias_ev):
                        if k == 0 and batch[0] == 0:
                            scalar.wait_ge(vM.h, sbias_ev)
                        pending = 0
                        for g in batch:
                            masked = g >= ngr - 2
                            psR.wait_full(scalar)
                            ptR.wait_free(scalar, ahead=pending)
                            ps = pS[psR.nc_ % 3]
                            pt = sPt[(ptR.np_ + pending) % 3]
                            sM.inc(scalar.activation(pt[:], ps[:], AF.Exp,
                                                     bias=sBias[:], scale=SCALE), 1)
                            psR.consumed(sM)
                            if not masked:
                                ptR.produced(sM)
                            else:
                                pending += 1

                    @on("vector")
                    def _(vector, h=h, k=k, batch=batch, ngr=ngr):
                        for g in batch:
                            if g < ngr - 2:
                                continue
                            vector.wait_ge(sM.h, sM.n - (batch[-1] - g))
                            if g == ngr - 2:
                                mkR.wait_full(vector)
                            pt = sPt[ptR.np_ % 3]
                            mof = (g - (ngr - 2)) * 1024
                            vM.inc(vector.tensor_tensor(
                                pt[:], pt[:],
                                sMk[mkR.nc_ % 2][:, mof:mof + 1024],
                                op=ALU.mult), 1)
                            ptR.produced(vM)
                            if g == ngr - 1:
                                mkR.consumed(vM)

                    @on("tensor")
                    def _(tensor, h=h, k=k, batch=batch, ngr=ngr):
                        for g in batch:
                            if g == 0:
                                tensor.wait_ge(vM.h, vM.n)
                            ptR.wait_full(tensor)
                            pt = sPt[ptR.nc_ % 3]
                            first = g == 0
                            ins = None
                            for i in range(4):
                                tensor.matmul(pL, sOnesC[:],
                                              pt[:, i * 256:(i + 1) * 256],
                                              start=(first and i == 0),
                                              stop=(g == ngr - 1 and i == 3))
                            for i in range(4):
                                c = 4 * g + i
                                ins = tensor.matmul(
                                    pOut[:], sV[h % 2][:, c * 128:(c + 1) * 128],
                                    pt[:, i * 256:(i + 1) * 256],
                                    start=(first and i == 0),
                                    stop=(g == ngr - 1 and i == 3))
                            tM.inc(ins, 1)
                            ptR.consumed(tM)
                            if g == ngr - 1 and k == NBLK - 1:
                                kR.consumed(tM)
                                qR.consumed(tM)
                                vR.consumed(tM)

                @on("vector")
                def _(vector, h=h, k=k):
                    s, v = ptR.cv[-1]
                    vector.wait_ge(s.h, v)
                    vector.wait_ge(vM.h, vM.n)
                    vM.inc(vector.reciprocal(sRL[:], pL), 1)

                @on("tensor")
                def _(tensor, h=h, k=k):
                    tensor.wait_ge(vM.h, vM.n)
                    tM.inc(tensor.matmul(pBc, sOnesF[:], sRL[:],
                                         start=True, stop=True), 1)

                @on("vector")
                def _(vector, h=h, k=k):
                    vector.wait_ge(tM.h, tM.n)
                    vM.inc(vector.tensor_copy(sBcB[:], pBc), 1)
                    vector.wait_ge(vM.h, vM.n)
                    aoR.wait_free(vector)
                    vM.inc(vector.tensor_tensor(sAo[aoR.np_ % 2][:],
                                                pOut[:], sBcB[:], op=ALU.mult), 1)
                    aoR.produced(vM)

                @on("sync")
                def _(sync, h=h, k=k):
                    aoR.wait_full(sync)
                    dS.inc(sync.dma_start(out=ascr[h][:, k * 256:(k + 1) * 256],
                                          in_=sAo[aoR.nc_ % 2][:]), 16)
                    aoR.consumed(dS)

        if DEBUG >= 2:
            barrier()

            @on("sync")
            def _(sync):
                for hh in range(H):
                    dS.inc(sync.dma_start(out=dbg_at[hh], in_=ascr[hh]), 16)
                sync.wait_ge(dS.h, dS.n)

        barrier()
        ctxB1.close()
        ctxB.close()

        ctxP2 = ExitStack()
        e2 = ctxP2.enter_context
        sWo = e2(nc.sbuf_tensor("sWo", [128, NCH * D], BF16))
        stgB = [e2(nc.sbuf_tensor(f"stgB{i}", [128, D], F32)) for i in range(2)]
        sXm = [e2(nc.sbuf_tensor(f"sXm{i}", [128, D], F32)) for i in range(2)]
        sX1 = [e2(nc.sbuf_tensor(f"sX1{i}", [128, D], F32)) for i in range(2)]
        sH2 = e2(nc.sbuf_tensor("sH2", [128, D], BF16))
        sH2T = e2(nc.sbuf_tensor("sH2T", [128, NCH * 128], BF16))
        sAt = [e2(nc.sbuf_tensor(f"sAt{i}", [128, 128], BF16)) for i in range(16)]
        sPj = [e2(nc.sbuf_tensor(f"sPj{i}", [128, D], F32)) for i in range(2)]
        sLgT = e2(nc.sbuf_tensor("sLgT", [32, 128], F32))
        sLg = e2(nc.sbuf_tensor("sLg", [128, 32], F32))
        sU = e2(nc.sbuf_tensor("sU", [128, 32], F32))
        sUc = e2(nc.sbuf_tensor("sUc", [128, 32], F32))
        sMsk = e2(nc.sbuf_tensor("sMsk", [128, 32], F32))
        sM6 = e2(nc.sbuf_tensor("sM6", [128, 6], F32))
        sSum = e2(nc.sbuf_tensor("sSum", [128, 1], F32))
        sGt = e2(nc.sbuf_tensor("sGt", [128, 32], F32))
        ctxP2p = ExitStack()
        pPj = ctxP2p.enter_context(nc.psum_tensor("pPj", [128, 2048], F32))
        pTc = ctxP2p.enter_context(nc.psum_tensor("pTc", [128, 512], BF16))
        pTd = ctxP2p.enter_context(nc.psum_tensor("pTd", [128, 512], BF16))
        pR32 = ctxP2p.enter_context(nc.psum_tensor("pR32", [32, 128], F32))
        pTf = ctxP2p.enter_context(nc.psum_tensor("pTf", [128, 32], F32))

        emit_wcast(wo, sWo, stgB)

        @on("tensor")
        def _(tensor):
            tensor.wait_ge(pM.h, pM.n)

        xmR = Ring(2)
        atR = Ring(16)
        x1R = Ring(2)
        pjR = Ring(1)
        pjsR = Ring(2)
        h2R = Ring(1)
        h2tR = Ring(1)
        tp2R = Ring(2)
        h2dma = [0]

        for ti in range(16):
            @on("sync")
            def _(sync, ti=ti):
                xmR.wait_free(sync)
                dS.inc(sync.dma_start(out=sXm[ti % 2][:],
                                      in_=xmine[ti * 128:(ti + 1) * 128, :]), 16)
                xmR.produced(dS)
                for ch in range(NCH):
                    atR.wait_free(sync)
                    dS.inc(sync.dma_start(
                        out=sAt[atR.np_ % 16][:],
                        in_=ascr[ch][:, ti * 128:(ti + 1) * 128]), 16)
                    atR.produced(dS)

            @on("tensor")
            def _(tensor, ti=ti):
                pjR.wait_free(tensor)
                for db in range(4):
                    tensor.matmul(pPj[:, db * 512:(db + 1) * 512], sOnesB[:],
                                  sBoR[:, db * 512:(db + 1) * 512],
                                  start=True, stop=False)
                ins = None
                for ch in range(NCH):
                    atR.wait_full(tensor)
                    for db in range(4):
                        ins = tensor.matmul(
                            pPj[:, db * 512:(db + 1) * 512], sAt[atR.nc_ % 16][:],
                            sWo[:, ch * D + db * 512: ch * D + (db + 1) * 512],
                            start=False, stop=(ch == NCH - 1))
                    tM.inc(ins, 1)
                    atR.consumed(tM)
                pjR.produced(tM)

            @on("vector")
            def _(vector, ti=ti):
                pjR.wait_full(vector)
                pjsR.wait_free(vector)
                vector.wait_ge(vM.h, vM.n)
                vM.inc(vector.tensor_copy(sPj[pjsR.np_ % 2][:], pPj[:]), 1)
                pjR.consumed(vM)
                pjsR.produced(vM)

            emit_ln_pre(lambda ti=ti: sPj[(pjsR.np_ - 1) % 2][:])

            @on("vector")
            def _(vector, ti=ti):
                vector.wait_ge(sM.h, sM.n)
                xmR.wait_full(vector)
                x1R.wait_free(vector)
                s1 = sX1[ti % 2]
                vM.inc(vector.tensor_scalar(s1[:], sPj[(pjsR.np_ - 1) % 2][:],
                                            sT1[:], sT0[:],
                                            op0=ALU.add, op1=ALU.mult), 1)
                vector.wait_ge(vM.h, vM.n)
                vM.inc(vector.tensor_tensor(s1[:], s1[:], sXm[ti % 2][:],
                                            op=ALU.add), 1)
                xmR.consumed(vM)
                pjsR.consumed(vM)
                x1R.produced(vM)

            emit_ln_pre(lambda ti=ti: sX1[ti % 2][:])

            @on("vector")
            def _(vector, ti=ti):
                vector.wait_ge(sM.h, sM.n)
                h2R.wait_free(vector)
                vM.inc(vector.tensor_scalar(sH2[:], sX1[ti % 2][:],
                                            sT1[:], sT0[:],
                                            op0=ALU.add, op1=ALU.mult), 1)
                h2R.produced(vM)

            @on("sync")
            def _(sync, ti=ti):
                x1R.wait_full(sync)
                dS.inc(sync.dma_start(out=x1s[ti], in_=sX1[ti % 2][:]), 16)
                x1R.consumed(dS)

            for ch in range(NCH):
                @on("tensor")
                def _(tensor, ti=ti, ch=ch):
                    if ch == 0:
                        h2R.wait_full(tensor)
                        h2tR.wait_free(tensor)
                    tp2R.wait_free(tensor)
                    ins = tensor.transpose(
                        (pTc if tp2R.np_ % 2 == 0 else pTd)[:, 0:128],
                        sH2[:, ch * 128:(ch + 1) * 128], sIdent[:])
                    tM.inc(ins, 1)
                    tp2R.produced(tM)
                    if ch == NCH - 1:
                        h2R.consumed(tM)

                @on("vector")
                def _(vector, ti=ti, ch=ch):
                    if ch == 0:
                        vector.wait_ge(dS.h, h2dma[0])
                    tp2R.wait_full(vector)
                    vM.inc(vector.tensor_copy(
                        sH2T[:, ch * 128:(ch + 1) * 128],
                        (pTc if tp2R.nc_ % 2 == 0 else pTd)[:, 0:128]), 1)
                    tp2R.consumed(vM)
                    if ch == NCH - 1:
                        h2tR.produced(vM)

            @on("sync")
            def _(sync, ti=ti):
                h2tR.wait_full(sync)
                dS.inc(sync.dma_start(
                    out=h2ts[:, :, ti * 128:(ti + 1) * 128].rearrange("c p t -> p c t"),
                    in_=sH2T[:].rearrange("p (c t) -> p c t", t=128)), 16)
            h2dma[0] = dS.n

            @on("tensor")
            def _(tensor, ti=ti):
                ins = None
                for ch in range(NCH):
                    ins = tensor.matmul(pR32[:], sRwB[:, ch * 32:(ch + 1) * 32],
                                        sH2T[:, ch * 128:(ch + 1) * 128],
                                        start=(ch == 0), stop=(ch == NCH - 1))
                tM.inc(ins, 1)
                h2tR.consumed(tM)

            @on("vector")
            def _(vector, ti=ti):
                vector.wait_ge(tM.h, tM.n)
                vM.inc(vector.tensor_scalar(sLgT[:], pR32[:], sRwbT[:], None,
                                            op0=ALU.add), 1)

            @on("tensor")
            def _(tensor, ti=ti):
                tensor.wait_ge(vM.h, vM.n)
                tM.inc(tensor.transpose(pTf[:], sLgT[:], sIdentF[0:32, 0:32]), 1)

            @on("vector")
            def _(vector, ti=ti):
                vector.wait_ge(tM.h, tM.n)
                vM.inc(vector.tensor_copy(sLg[:], pTf[:]), 1)
                vector.wait_ge(vM.h, vM.n)
                vM.inc(vector.reduce_max(sT1[:], sLg[:], axis=AX.X), 1)
                vector.wait_ge(vM.h, vM.n)
                vM.inc(vector.tensor_scalar(sT1[:], sT1[:], -1.0, None,
                                            op0=ALU.mult), 1)

            @on("scalar")
            def _(scalar, ti=ti):
                scalar.wait_ge(vM.h, vM.n)
                sM.inc(scalar.activation(sU[:], sLg[:], AF.Exp, bias=sT1[:]), 1)

            @on("vector")
            def _(vector, ti=ti):
                vector.wait_ge(sM.h, sM.n)

                def step(ins):
                    vM.inc(ins, 1)
                    vector.wait_ge(vM.h, vM.n)
                vector.memset(sSum[:], 0.0)
                step(vector.tensor_copy(sUc[:], sU[:]))
                for i in range(6):
                    step(vector.reduce_max(sM6[:, i:i + 1], sUc[:], axis=AX.X))
                    step(vector.tensor_tensor(sSum[:], sSum[:], sM6[:, i:i + 1],
                                              op=ALU.add))
                    if i < 5:
                        step(vector.tensor_scalar(sMsk[:], sUc[:],
                                                  sM6[:, i:i + 1],
                                                  None, op0=ALU.is_lt))
                        step(vector.tensor_tensor(sUc[:], sUc[:], sMsk[:],
                                                  op=ALU.mult))
                step(vector.tensor_scalar(sMsk[:], sU[:], sM6[:, 5:6], None,
                                          op0=ALU.is_ge))
                step(vector.reciprocal(sSum[:], sSum[:]))
                step(vector.tensor_scalar(sGt[:], sU[:], sSum[:], None,
                                          op0=ALU.mult))
                vM.inc(vector.tensor_tensor(sGt[:], sGt[:], sMsk[:],
                                            op=ALU.mult), 1)

            @on("tensor")
            def _(tensor, ti=ti):
                tensor.wait_ge(vM.h, vM.n)
                tM.inc(tensor.transpose(pR32[:], sGt[:], sIdentF[:]), 1)

            @on("vector")
            def _(vector, ti=ti):
                vector.wait_ge(tM.h, tM.n)
                vM.inc(vector.tensor_copy(sGatesT[:, ti * 128:(ti + 1) * 128],
                                          pR32[:]), 1)

        if DEBUG >= 3:
            barrier()

            @on("sync")
            def _(sync):
                for tt in range(16):
                    dS.inc(sync.dma_start(out=dbg_x1[tt], in_=x1s[tt]), 16)
                for cc in range(NCH):
                    dS.inc(sync.dma_start(out=dbg_h2t[cc], in_=h2ts[cc]), 16)
                sync.wait_ge(dS.h, dS.n)

            @on("scalar")
            def _(scalar):
                scalar.wait_ge(vM.h, vM.n)
                dC.inc(scalar.dma_start(out=dbg_g, in_=sGatesT[:]), 16)
                scalar.wait_ge(dC.h, dC.n)

        barrier()
        ctxP2p.close()
        ctxP2.close()

        ctxC = ExitStack()
        eC = ctxC.enter_context
        sAcc = eC(nc.sbuf_tensor("sAcc", [128, 4 * D], F32))
        sG = eC(nc.sbuf_tensor("sG", [128, NHT * 512], BF16))
        sW2 = eC(nc.sbuf_tensor("sW2", [128, NHT * D], BF16))
        sW1g = [eC(nc.sbuf_tensor(f"sW1g{i}", [128, 2 * D], BF16)) for i in range(2)]
        stgC = [eC(nc.sbuf_tensor(f"stgC{i}", [128, D], F32)) for i in range(2)]
        sH2b = eC(nc.sbuf_tensor("sH2b", [128, NCH * 512], BF16))
        sGb = [eC(nc.sbuf_tensor(f"sGb{i}", [128, 512], BF16)) for i in range(2)]
        sRelu = [eC(nc.sbuf_tensor(f"sRelu{i}", [128, 512], BF16)) for i in range(2)]
        sX1t = eC(nc.sbuf_tensor("sX1t", [128, D], F32))
        sX2 = eC(nc.sbuf_tensor("sX2", [128, D], F32))

        pA10 = ctxC.enter_context(nc.psum_tensor("pA10", [128, 512], F32))
        pA11 = ctxC.enter_context(nc.psum_tensor("pA11", [128, 512], F32))
        pA20 = ctxC.enter_context(nc.psum_tensor("pA20", [128, 512], F32))
        pA21 = ctxC.enter_context(nc.psum_tensor("pA21", [128, 512], F32))
        pG = ctxC.enter_context(nc.psum_tensor("pG", [128, 512], F32))

        w1R = Ring(2)
        w2R = Ring(1)
        stcR = Ring(2)
        h2bR = Ring(1)
        gbR = Ring(2)
        grR = Ring(1)
        pa1R = Ring(2)
        pa2R = Ring(2)
        reR = Ring(2)
        x1tR = Ring(1)
        x2R = Ring(1)

        NEXP = 34
        w1save_ev = []
        w2sv = []

        def wsrc1(ei):
            return sw1[ei] if ei < 2 else rw1[ei - 2]

        def wsrc2(ei):
            return sw2[ei] if ei < 2 else rw2[ei - 2]

        for qp in range(4):
            for ei in range(NEXP):
                routed = ei >= 2

                @on("scalar")
                def _(scalar, qp=qp, ei=ei, routed=routed):
                    if routed:
                        grR.wait_free(scalar)
                        dC.inc(scalar.dma_start(
                            out=sGRow[:],
                            in_=sGatesT[ei - 2:ei - 1, qp * 512:(qp + 1) * 512]), 16)
                        grR.produced(dC)
                    h2bR.wait_free(scalar)
                    dC.inc(scalar.dma_start(
                        out=sH2b[:],
                        in_=h2ts[:, :, qp * 512:(qp + 1) * 512]
                        .rearrange("c p t -> p c t")), 16)
                    h2bR.produced(dC)

                GRPS = [(0, 2), (2, 2), (4, 2), (6, 2), (8, 2), (10, 1)]

                def emit_w1_group(gix, qp=qp, ei=ei):
                    ht0, gn = GRPS[gix]
                    if qp == 0:
                        @on("sync")
                        def _(sync, ht0=ht0, gn=gn, ei=ei):
                            for j in range(gn):
                                stcR.wait_free(sync)
                                dS.inc(sync.dma_start(
                                    out=stgC[stcR.np_ % 2][:, 0:NCH * 128],
                                    in_=wsrc1(ei)[:, (ht0 + j) * 128:(ht0 + j + 1) * 128]
                                    .rearrange("(c p) f -> p c f", p=128)), 16)
                                stcR.produced(dS)

                        @on("vector")
                        def _(vector, gn=gn):
                            vector.wait_ge(vM.h, vM.n)
                            for j in range(gn):
                                stcR.wait_full(vector)
                                if j == 0:
                                    w1R.wait_free(vector)
                                    i = w1R.np_
                                    if i >= 2 and w1save_ev[i - 2] is not None:
                                        vector.wait_ge(dC.h, w1save_ev[i - 2])
                                vM.inc(vector.tensor_copy(
                                    sW1g[w1R.np_ % 2][:, j * D:(j + 1) * D],
                                    stgC[stcR.nc_ % 2][:, 0:NCH * 128]), 1)
                                stcR.consumed(vM)
                                if j == gn - 1:
                                    w1R.produced(vM)

                        @on("scalar")
                        def _(scalar, gix=gix, ei=ei):
                            s, v = w1R.pv[-1]
                            scalar.wait_ge(s.h, v)
                            dC.inc(scalar.dma_start(
                                out=wscr1[ei][gix],
                                in_=sW1g[(w1R.np_ - 1) % 2][:, 0:4096]), 16)
                            w1save_ev.append(dC.n)
                    else:
                        @on("sync")
                        def _(sync, gix=gix, ei=ei):
                            w1R.wait_free(sync)
                            i = w1R.np_
                            if i >= 2 and w1save_ev[i - 2] is not None:
                                sync.wait_ge(dC.h, w1save_ev[i - 2])
                            dS.inc(sync.dma_start(
                                out=sW1g[w1R.np_ % 2][:, 0:4096],
                                in_=wscr1[ei][gix]), 16)
                            w1R.produced(dS)
                            w1save_ev.append(None)

                def emit_w2(qp=qp, ei=ei):
                    if qp == 0:
                        for hc in range(NHT):
                            @on("sync")
                            def _(sync, hc=hc, ei=ei):
                                stcR.wait_free(sync)
                                dS.inc(sync.dma_start(
                                    out=stgC[stcR.np_ % 2][:],
                                    in_=wsrc2(ei)[hc * 128:(hc + 1) * 128, :]), 16)
                                stcR.produced(dS)

                            @on("vector")
                            def _(vector, hc=hc):
                                stcR.wait_full(vector)
                                if hc == 0:
                                    w2R.wait_free(vector)
                                    if w2sv:
                                        vector.wait_ge(dC.h, w2sv[-1])
                                vector.wait_ge(vM.h, vM.n)
                                vM.inc(vector.tensor_copy(
                                    sW2[:, hc * D:(hc + 1) * D],
                                    stgC[stcR.nc_ % 2][:]), 1)
                                stcR.consumed(vM)
                                if hc == NHT - 1:
                                    w2R.produced(vM)

                        @on("scalar")
                        def _(scalar, ei=ei):
                            s, v = w2R.pv[-1]
                            scalar.wait_ge(s.h, v)
                            dC.inc(scalar.dma_start(out=wscr2[ei], in_=sW2[:]), 16)
                            w2sv.append(dC.n)
                    else:
                        @on("sync")
                        def _(sync, ei=ei):
                            w2R.wait_free(sync)
                            if w2sv:
                                sync.wait_ge(dC.h, w2sv[-1])
                            dS.inc(sync.dma_start(out=sW2[:], in_=wscr2[ei]), 16)
                            w2R.produced(dS)

                if routed:
                    @on("tensor")
                    def _(tensor, qp=qp, ei=ei):
                        grR.wait_full(tensor)
                        gbR.wait_free(tensor)
                        tM.inc(tensor.matmul(pG[:], sOnesB[:], sGRow[:],
                                             start=True, stop=True), 1)

                    @on("scalar")
                    def _(scalar, qp=qp, ei=ei):
                        scalar.wait_ge(tM.h, tM.n)
                        sM.inc(scalar.copy(sGb[gbR.np_ % 2][:], pG[:]), 1)
                        gbR.produced(sM)
                        grR.consumed(sM)

                for ht in range(NHT):
                    gix, jj = divmod(ht, 2)
                    last_of_grp = (ht % 2 == 1) or ht == NHT - 1
                    if jj == 0:
                        emit_w1_group(gix)

                    @on("tensor")
                    def _(tensor, qp=qp, ei=ei, ht=ht, gix=gix, jj=jj,
                          last_of_grp=last_of_grp):
                        if ht == 0:
                            h2bR.wait_full(tensor)
                        if jj == 0:
                            w1R.wait_full(tensor)
                        pa1R.wait_free(tensor)
                        pa = pA10 if pa1R.np_ % 2 == 0 else pA11
                        w1s = sW1g[w1R.nc_ % 2]
                        ins = None
                        for ch in range(NCH):
                            ins = tensor.matmul(
                                pa[:], w1s[:, jj * D + ch * 128: jj * D + (ch + 1) * 128],
                                sH2b[:, ch * 512:(ch + 1) * 512],
                                start=(ch == 0), stop=(ch == NCH - 1))
                        tM.inc(ins, 1)
                        if last_of_grp:
                            w1R.consumed(tM)
                        pa1R.produced(tM)
                        if ht == NHT - 1:
                            h2bR.consumed(tM)

                    @on("scalar")
                    def _(scalar, qp=qp, ei=ei, ht=ht, routed=routed):
                        pa1R.wait_full(scalar)
                        pa = pA10 if pa1R.nc_ % 2 == 0 else pA11
                        if routed:
                            reR.wait_free(scalar)
                            sM.inc(scalar.activation(
                                sRelu[reR.np_ % 2][:], pa[:], AF.Relu,
                                bias=sRb1T[:, ht * 32 + ei - 2: ht * 32 + ei - 1]), 1)
                            pa1R.consumed(sM)
                            reR.produced(sM)
                        else:
                            sM.inc(scalar.activation(
                                sG[:, ht * 512:(ht + 1) * 512],
                                pa[:], AF.Relu, scale=0.5,
                                bias=sHsb1T[:, ht * 2 + ei: ht * 2 + ei + 1]), 1)
                            pa1R.consumed(sM)

                    if routed:
                        @on("vector")
                        def _(vector, qp=qp, ei=ei, ht=ht):
                            reR.wait_full(vector)
                            if ht == 0:
                                gbR.wait_full(vector)
                            vM.inc(vector.tensor_tensor(
                                sG[:, ht * 512:(ht + 1) * 512],
                                sRelu[reR.nc_ % 2][:], sGb[(gbR.np_ - 1) % 2][:],
                                op=ALU.mult), 1)
                            reR.consumed(vM)
                            if ht == NHT - 1:
                                gbR.consumed(vM)

                emit_w2()

                @on("tensor")
                def _(tensor, qp=qp, ei=ei, routed=routed):
                    if routed:
                        tensor.wait_ge(vM.h, vM.n)
                    else:
                        tensor.wait_ge(sM.h, sM.n)
                    w2R.wait_full(tensor)

                for tt in range(4):
                    for db in range(4):
                        @on("tensor")
                        def _(tensor, qp=qp, ei=ei, tt=tt, db=db, routed=routed):
                            pa2R.wait_free(tensor)
                            pa = pA20 if pa2R.np_ % 2 == 0 else pA21
                            started = False
                            if ei == 0:
                                tensor.matmul(pa[:], sOnesB[:],
                                              sSb2c[:, db * 512:(db + 1) * 512],
                                              start=True, stop=False)
                                started = True
                            ins = None
                            for hc in range(NHT):
                                ins = tensor.matmul(
                                    pa[:],
                                    sG[:, hc * 512 + tt * 128: hc * 512 + (tt + 1) * 128],
                                    sW2[:, hc * D + db * 512: hc * D + (db + 1) * 512],
                                    start=(not started and hc == 0),
                                    stop=(hc == NHT - 1 and ei != NEXP - 1))
                            if ei == NEXP - 1:
                                ins = tensor.matmul(
                                    pa[:],
                                    sGatesT[:, qp * 512 + tt * 128: qp * 512 + (tt + 1) * 128],
                                    sRb2B[:, db * 512:(db + 1) * 512],
                                    start=False, stop=True)
                            tM.inc(ins, 1)
                            pa2R.produced(tM)
                            if tt == 3 and db == 3:
                                w2R.consumed(tM)

                        @on("vector")
                        def _(vector, qp=qp, ei=ei, tt=tt, db=db):
                            pa2R.wait_full(vector)
                            pa = pA20 if pa2R.nc_ % 2 == 0 else pA21
                            dst = sAcc[:, tt * D + db * 512: tt * D + (db + 1) * 512]
                            if ei == 0:
                                vM.inc(vector.tensor_copy(dst, pa[:]), 1)
                            else:
                                vM.inc(vector.tensor_tensor(dst, pa[:], dst,
                                                            op=ALU.add), 1)
                            pa2R.consumed(vM)

            for tt in range(4):
                ti = qp * 4 + tt

                @on("sync")
                def _(sync, ti=ti):
                    x1tR.wait_free(sync)
                    dS.inc(sync.dma_start(out=sX1t[:], in_=x1s[ti]), 16)
                    x1tR.produced(dS)

                emit_ln_pre(lambda tt=tt: sAcc[:, tt * D:(tt + 1) * D])

                @on("vector")
                def _(vector, ti=ti, tt=tt):
                    vector.wait_ge(sM.h, sM.n)
                    x1tR.wait_full(vector)
                    x2R.wait_free(vector)
                    vM.inc(vector.tensor_scalar(sX2[:], sAcc[:, tt * D:(tt + 1) * D],
                                                sT1[:], sT0[:],
                                                op0=ALU.add, op1=ALU.mult), 1)
                    vector.wait_ge(vM.h, vM.n)
                    vM.inc(vector.tensor_tensor(sX2[:], sX2[:], sX1t[:],
                                                op=ALU.add), 1)
                    x1tR.consumed(vM)
                    x2R.produced(vM)

                @on("sync")
                def _(sync, ti=ti):
                    x2R.wait_full(sync)
                    dS.inc(sync.dma_start(out=y[ti * 128:(ti + 1) * 128, :],
                                          in_=sX2[:]), 16)
                    x2R.consumed(dS)

        @on("sync")
        def _(sync):
            sync.wait_ge(dS.h, dS.n)
            sync.wait_ge(dC.h, dC.n)

        ctxC.close()

    return nc


_NC_CACHE = {}


def _get_nc():
    key = DEBUG
    if key not in _NC_CACHE:
        _NC_CACHE[key] = build()
    return _NC_CACHE[key]


def _host_inputs(inputs):
    x = np.asarray(inputs["L0_attn_x"], dtype=np.float32)
    identN = np.eye(128).astype(ml_dtypes.bfloat16)
    base = dict(
        wq=np.asarray(inputs["L0_attn_Wq"], np.float32),
        wk=np.asarray(inputs["L0_attn_Wk"], np.float32),
        wv=np.asarray(inputs["L0_attn_Wv"], np.float32),
        wo=np.asarray(inputs["L0_attn_Wo"], np.float32),
        bq=np.asarray(inputs["L0_attn_bq"], np.float32).reshape(1, D),
        bk=np.asarray(inputs["L0_attn_bk"], np.float32).reshape(1, D),
        bv=np.asarray(inputs["L0_attn_bv"], np.float32).reshape(1, D),
        bo=np.asarray(inputs["L0_attn_bo"], np.float32).reshape(1, D),
        sw1=np.asarray(inputs["L0_ffn_sw1"], np.float32),
        sb1=np.asarray(inputs["L0_ffn_sb1"], np.float32),
        sw2=np.asarray(inputs["L0_ffn_sw2"], np.float32),
        sb2=np.asarray(inputs["L0_ffn_sb2"], np.float32),
        rw1=np.asarray(inputs["L0_ffn_rw1"], np.float32),
        rb1=np.asarray(inputs["L0_ffn_rb1"], np.float32),
        rw2=np.asarray(inputs["L0_ffn_rw2"], np.float32),
        rb2=np.asarray(inputs["L0_ffn_rb2"], np.float32),
        rww=np.asarray(inputs["L0_ffn_router_w"], np.float32),
        rwb=np.asarray(inputs["L0_ffn_router_b"], np.float32).reshape(1, 32),
        ident=identN,
    )
    in_maps = []
    mines = []
    for c in range(8):
        B, q = c // 4, c % 4
        xbat = np.ascontiguousarray(x[B])
        ks = np.arange(NBLK)
        t0s = QW * (q + 4 * ks)
        hcol = np.arange(H)[:, None, None] * 512
        slab = (t0s // 16)[None, :, None]
        rho = np.arange(16)[None, None, :]
        bandrows = (hcol + slab + rho).reshape(-1)
        minerows = (t0s[:, None] + np.arange(QW)[None, :]).reshape(-1)
        mines.append((B, minerows))
        p = np.arange(128)
        msk = np.zeros((NBLK, 128, 2048), np.float32)
        for k in range(NBLK):
            t0 = QW * (q + 4 * k)
            for cc in range(8):
                s0 = 128 * (8 * k + cc)
                jj = np.arange(QW)
                msk[k, :, cc * QW:(cc + 1) * QW] = (
                    (s0 + p[:, None]) <= (t0 + jj[None, :]))
        in_maps.append(dict(
            base,
            xb=xbat,
            xband=np.ascontiguousarray(xbat[bandrows]),
            xmine=np.ascontiguousarray(xbat[minerows]),
            masks=msk.astype(ml_dtypes.bfloat16),
        ))
    return x, in_maps, mines


def kernel(**inputs):
    x, in_maps, mines = _host_inputs(inputs)
    nc = _get_nc()
    res = run_bass_kernel_spmd(nc, in_maps, core_ids=list(range(8)))
    out = np.empty_like(x)
    for c in range(8):
        B, minerows = mines[c]
        out[B, minerows, :] = np.asarray(res.results[c]["y"], np.float32)
    return out


if __name__ == "__main__":
    import reference as R
    ins = {k: np.asarray(v) for k, v in R.setup_inputs(0).items()}
    o = kernel(**ins)
    print("out", o.shape, o.dtype)
